# version 18
# baseline (speedup 1.0000x reference)
import os
import numpy as np
import ml_dtypes
import concourse.bass as bass
import concourse.mybir as mybir
from concourse.bass_utils import run_bass_kernel_spmd

F32 = mybir.dt.float32
BF16 = mybir.dt.bfloat16
I8 = mybir.dt.int8
AF = mybir.ActivationFunctionType
ALU = mybir.AluOpType
AX = mybir.AxisListType

D = 2048
H, HD = 32, 64
NDC = 16
PT = 512
NTILE = 4
T = 2048
NIC = 64
EPS = 1e-5
ARENA = 212832

ENGINES = ("tensor", "vector", "scalar", "gpsimd", "sync")


class _StopEmit(Exception):
    pass


class Prog:
    def __init__(self, nc):
        self.nc = nc
        self.ops = {e: [] for e in ENGINES}
        self.c = {}
        self.sem_handles = {}
        self._g = []

    def make_sems(self, names):
        for n in names:
            g = self.nc.semaphore(n)
            self._g.append(g)
            self.sem_handles[n] = g.__enter__()
            self.c[n] = 0

    def emit(self, eng, fn, waits=(), inc=None, amt=1):
        w = [(s, v) for (s, v) in waits if v > 0]
        import os
        if os.environ.get("KTRACE"):
            import traceback
            frs = traceback.extract_stack(limit=10)
            lines = [f.lineno for f in frs if "kernel" in (f.filename or "")
                     and f.lineno > 300]
            self.ops[eng].append((fn, w, inc, amt, str(lines[-3:])))
        else:
            self.ops[eng].append((fn, w, inc, amt))
        if inc is not None:
            self.c[inc] += amt
        return self.c.get(inc)

    def val(self, sem):
        return self.c[sem]

    def install(self, block):
        sems = self.sem_handles

        def runner(name):
            def _f(eng):
                for fn, waits, inc, amt, *_ in self.ops[name]:
                    for s, v in waits:
                        eng.wait_ge(sems[s], v)
                    inst = fn(eng)
                    if inc is not None:
                        inst.then_inc(sems[inc], amt)
            return _f
        block.tensor(runner("tensor"))
        block.vector(runner("vector"))
        block.scalar(runner("scalar"))
        block.gpsimd(runner("gpsimd"))
        block.sync(runner("sync"))

    def close(self):
        for g in reversed(self._g):
            g.__exit__(None, None, None)


def build(stop=None):
    nc = bass.Bass("TRN2", target_bir_lowering=False,
                   detect_race_conditions=False)

    x_in = nc.dram_tensor("x", [T, D], F32, kind="ExternalInput").ap()
    w_proj = {}
    for nm in ("wq", "wk", "wv", "wg", "wo"):
        w_proj[nm] = nc.dram_tensor(nm, [2, NDC, 128, 1024], F32,
                                    kind="ExternalInput").ap()
    w1_in = nc.dram_tensor("w1", [128, NDC * 160], F32, kind="ExternalInput").ap()
    w2_in = nc.dram_tensor("w2", [32, 5 * D], F32, kind="ExternalInput").ap()
    dw1_in = nc.dram_tensor("dw1", [128, NDC * 64], F32, kind="ExternalInput").ap()
    dw2_in = nc.dram_tensor("dw2", [64, D], F32, kind="ExternalInput").ap()
    maa6_in = nc.dram_tensor("maa6", [128, 96], F32, kind="ExternalInput").ap()
    td_in = nc.dram_tensor("td", [128, 16], F32, kind="ExternalInput").ap()
    bg_in = nc.dram_tensor("bg", [128, 16], F32, kind="ExternalInput").ap()
    sg_in = nc.dram_tensor("sg", [128, 64], F32, kind="ExternalInput").ap()
    su_in = nc.dram_tensor("su", [128, 64], F32, kind="ExternalInput").ap()
    sd_in = nc.dram_tensor("sd", [1, D], F32, kind="ExternalInput").ap()
    wg8_in = nc.dram_tensor("wg8", [NIC, 128, NDC * 128], I8,
                            kind="ExternalInput").ap()
    wu8_in = nc.dram_tensor("wu8", [NIC, 128, NDC * 128], I8,
                            kind="ExternalInput").ap()
    wd8_in = nc.dram_tensor("wd8", [8192, D], I8, kind="ExternalInput").ap()
    id_in = nc.dram_tensor("idm", [128, 128], F32, kind="ExternalInput").ap()
    mask_in = nc.dram_tensor("maskm", [128, 128], F32, kind="ExternalInput").ap()
    out_hbm = nc.dram_tensor("out", [T, D], F32, kind="ExternalOutput").ap()
    w_scr = nc.dram_tensor("w_scr", [5, 2, NDC, 128, 1024], BF16,
                           kind="Internal").ap()
    wo_scr = w_scr[4]
    scr_ix = {"wq": 0, "wk": 1, "wv": 2, "wg": 3}

    ctx = []

    def enter(g):
        ctx.append(g)
        return g.__enter__()

    arena = enter(nc.sbuf_tensor("arena", [128, ARENA], I8))
    ps = [enter(nc.psum_tensor(f"ps{i}", [128, 512], F32)) for i in range(8)]

    off = [0]

    def alloc(n, align=64):
        o = (off[0] + align - 1) // align * align
        off[0] = o + n
        return o

    def V(o, nbytes, dtype, parts=128, p0=0):
        return arena[p0:p0 + parts, o:o + nbytes].bitcast(dtype)

    o_x2 = alloc(32768)
    o_S = alloc(4096)
    o_Sbf = alloc(2048)
    o_id = alloc(256)
    o_mask = alloc(256)
    o_td = alloc(64)
    o_maa = alloc(384)
    o_bg = alloc(64)
    o_sg = alloc(256)
    o_su = alloc(256)
    o_sdb = alloc(8192)
    o_dw1 = alloc(2048)
    o_dw2 = alloc(4096)
    o_w1 = alloc(5120)
    o_w2b = alloc(4096)
    o_t6 = alloc(1024)
    o_t5 = alloc(5120)
    o_stat = alloc(128)
    o_enpc = alloc(128)
    o_eps = alloc(64)
    o_carry = alloc(64)
    o_wbuf = [alloc(4096), alloc(4096)]
    o_scrA = alloc(2048)
    o_scrB = alloc(2048)
    o_scrC = alloc(2048)
    o_dump = alloc(4096)
    o_ext = alloc(16384)
    o_d1 = alloc(16384)
    o_xf = alloc(16384)
    o_q = alloc(16384)
    o_k = alloc(16384)
    o_v = alloc(16384)
    o_g = alloc(16384)
    o_lt = alloc(8192)
    assert off[0] <= ARENA, f"arena overflow {off[0]}"

    o_gwb = [o_d1, o_d1 + 4096]
    o_uwb = [o_d1 + 8192, o_d1 + 12288]
    o_wdb = [o_lt, o_lt + 4096]
    o_mixt = [o_lt, o_lt + 1024]
    o_sgb = [o_scrA, o_scrB]

    x2 = V(o_x2, 32768, F32)
    Sv = V(o_S, 4096, F32)
    Sb = V(o_Sbf, 2048, BF16)
    idm = V(o_id, 256, BF16)
    maskm = V(o_mask, 256, BF16)
    tdv = V(o_td, 64, F32)
    maa = V(o_maa, 384, F32)
    bgv = V(o_bg, 64, F32)
    sgv = V(o_sg, 256, F32)
    suv = V(o_su, 256, F32)
    sdb = V(o_sdb, 8192, F32)
    dw1 = V(o_dw1, 2048, BF16)
    dw2 = V(o_dw2, 4096, BF16, parts=64)
    w1v = V(o_w1, 5120, BF16)
    w2b = V(o_w2b, 4096, BF16, parts=32)
    t6v = V(o_t6, 1024, BF16, parts=64)
    t5v = V(o_t5, 5120, BF16, parts=32)
    stat = V(o_stat, 128, F32)
    enpc = V(o_enpc, 64, F32)
    epsv = V(o_eps, 4, F32)
    carry = V(o_carry, 32, BF16)
    scrA = V(o_scrA, 2048, F32)
    scrB = V(o_scrB, 2048, F32)
    scrC = V(o_scrC, 2048, F32)
    dump = V(o_dump, 4096, BF16)
    dump2 = V(o_scrA, 4096, BF16)
    Pv = V(o_ext, 8192, F32)
    kdtm = V(o_ext + 8192, 4096, BF16)
    ffn_tm = V(o_ext, 16384, BF16)
    attn_tm = V(o_d1, 16384, BF16)
    vtm = V(o_v, 16384, BF16)

    def fm3(o, parts=128, p0=0):
        return V(o, 16384, BF16, parts, p0).rearrange("p (a b) -> p a b", a=16)

    ln1_3 = fm3(o_ext)
    d1_3 = fm3(o_d1)
    xf_3 = fm3(o_xf)
    q_3 = fm3(o_q)
    k_3 = fm3(o_k)
    g_3 = fm3(o_g)
    ln2_3 = fm3(o_g)

    def sl3(ap3, dc, a, b):
        return ap3[:, dc:dc + 1, a:b].squeeze(1)

    def fuse_ic(ic):
        base = [o_xf, o_q, o_k, o_v][ic // 16]
        return V(base + (ic % 16) * 1024, 1024, BF16)

    prog = Prog(nc)
    prog.make_sems(["T", "V", "A", "WC", "XC", "XI", "ST",
                    "W0", "W1", "W2", "W3", "W4", "W5", "W6", "W7", "W8",
                    "W9", "G0", "G1", "U0", "U1",
                    "D0", "D1", "D2", "D3", "D4", "D5", "P", "WB"])

    def mm(out, lhsT, rhs, start, stop, waits=(), inc=None, tp=None):
        def f(e, out=out, lhsT=lhsT, rhs=rhs, start=start, stop=stop, tp=tp):
            return e.matmul(out, lhsT, rhs, start=start, stop=stop,
                            tile_position=tp)
        return prog.emit("tensor", f, waits, inc)

    def tr(out, in_, waits=(), inc=None):
        def f(e, out=out, in_=in_):
            return e.transpose(out, in_, idm)
        return prog.emit("tensor", f, waits, inc)

    def ve(meth, *args, waits=(), inc=None, **kw):
        def f(e, meth=meth, args=args, kw=kw):
            return getattr(e, meth)(*args, **kw)
        return prog.emit("vector", f, waits, inc)

    def ac(meth, *args, waits=(), inc=None, **kw):
        def f(e, meth=meth, args=args, kw=kw):
            return getattr(e, meth)(*args, **kw)
        return prog.emit("scalar", f, waits, inc)

    def po(meth, *args, waits=(), inc=None, **kw):
        def f(e, meth=meth, args=args, kw=kw):
            return getattr(e, meth)(*args, **kw)
        return prog.emit("gpsimd", f, waits, inc)

    def dma_w(dst, src, waits=(), sem="WC"):
        def f(e, dst=dst, src=src):
            return e.dma_start(out=dst, in_=src)
        return prog.emit("gpsimd", f, waits, sem, 16)

    def dma_x(dst, src, waits=(), sem="XC"):
        def f(e, dst=dst, src=src):
            return e.dma_start(out=dst, in_=src)
        return prog.emit("sync", f, waits, sem, 16)

    def dma_wb(dst, src, waits=()):
        def f(e, dst=dst, src=src):
            return e.dma_start(out=dst, in_=src)
        return prog.emit("sync", f, waits, "WB", 16)

    def dma_st(dst, src, waits=()):
        def f(e, dst=dst, src=src):
            return e.dma_start(out=dst, in_=src)
        return prog.emit("sync", f, waits, "ST", 16)

    dbg_specs = []

    def stop_here(tag, pairs):
        if stop == tag:
            dbg_specs.extend(pairs)
            raise _StopEmit

    dbg_w = []
    if os.environ.get("KDBG0"):
        v0 = ve("memset", arena[:, 0:ARENA].bitcast(F32), 0.0, inc="V")
        dbg_w = [("V", v0)]
    dma_w(idm, id_in, waits=dbg_w)
    dma_w(maskm, mask_in)
    dma_w(dw1, dw1_in)
    dma_w(dw2, dw2_in)
    dma_w(w1v, w1_in)
    W_CONST = prog.val("WC")
    dma_x(tdv, td_in)
    dma_x(maa, maa6_in)
    dma_x(bgv, bg_in)
    dma_x(sgv, sg_in)
    dma_x(suv, su_in)
    sdrow = V(o_ext, 8192, F32, parts=1)
    ones1 = V(o_ext + 8192, 512, F32, parts=1)
    dma_x(sdrow, sd_in)
    X_CONST = prog.val("XC")
    ve("memset", ones1, 1.0)
    ve("memset", epsv, EPS)
    ve("memset", Sv, 0.0)
    ve("memset", Sb, 0.0)
    v_init = ve("memset", carry[:, 0:16], 0.0, inc="V")
    last_ev = 0
    for j in range(4):
        tmm = mm(ps[0][:, 0:512], ones1[0:1, 0:128],
                 sdrow[0:1, j * 512:(j + 1) * 512], True, True,
                 waits=[("XC", X_CONST), ("WC", W_CONST), ("V", v_init),
                        ("V", last_ev)],
                 inc="T")
        ve("tensor_copy", sdb[:, j * 512:(j + 1) * 512],
           ps[0][:, 0:512], waits=[("T", tmm)])
        last_ev = prog.emit("vector", lambda e: e.drain(), inc="V")
    SDB_V = last_ev

    n_stores = 0
    ms = {"ps67_free_V": SDB_V, "ps03_free_V": SDB_V, "sbf_V": 0,
          "ps67_free2_V": 0, "supd_free_V": 0, "t6_A": 0,
          "v_ssq": 0, "mix_stt": [0] * 8, "chainP": {},
          "chain_prevP": 0, "ps03_free": None}
    wbuf_last_T = [0, 0]
    gwb_last_T = [0, 0]
    uwb_last_T = [0, 0]
    ws_off = [o_wbuf[0] + 2048 * i for i in range(4)] + \
             [o_xf + 2048 * i for i in range(4)]
    WS_SEMS = ["W0", "W1", "W3", "W4", "W5", "W6", "W7", "W8"]
    ws_last_T = [0] * 8
    ws_wb_slot = [0] * 8
    wo_wb = {}
    w2_last_T = [0, 0]
    wd_off = [o_lt, o_lt + 4096, o_d1, o_d1 + 4096,
              o_d1 + 8192, o_d1 + 12288]
    WD_SEMS = ["D0", "D1", "D2", "D3", "D4", "D5"]
    wd_last_T = [0] * 6

    def ln_tc(src, tc, w0):
        a_sq = ac("activation", dump, src, AF.Square,
                  waits=w0 + [("V", ms["v_ssq"])], inc="A")
        ve("reduce_sum", stat[:, 4 + tc:5 + tc], src, AX.X, waits=w0)
        ve("reduce_sum", stat[:, tc:tc + 1], dump, AX.X,
           waits=[("A", a_sq)])
        v_ssq = prog.emit("vector", lambda e: e.drain(), inc="V")
        ms["v_ssq"] = v_ssq
        a_t1 = ac("activation", stat[:, 12 + tc:13 + tc],
                  stat[:, 4 + tc:5 + tc], AF.Square,
                  waits=[("V", v_ssq)], inc="A")
        ve("tensor_scalar", stat[:, 16 + tc:17 + tc],
           stat[:, 4 + tc:5 + tc], -1.0 / D, None, ALU.mult)
        v_var = ve("scalar_tensor_tensor", stat[:, 8 + tc:9 + tc],
                   stat[:, tc:tc + 1], float(D), stat[:, 12 + tc:13 + tc],
                   ALU.mult, ALU.subtract, waits=[("A", a_t1)], inc="V")
        a_std = ac("activation", stat[:, 8 + tc:9 + tc],
                   stat[:, 8 + tc:9 + tc], AF.Sqrt, bias=epsv,
                   scale=1.0 / (D * D), waits=[("V", v_var)], inc="A")
        ve("reciprocal", stat[:, 8 + tc:9 + tc], stat[:, 8 + tc:9 + tc],
           waits=[("A", a_std)])
        v_rc = prog.emit("vector", lambda e: e.drain(), inc="V")
        a_nm = ac("activation", stat[:, 12 + tc:13 + tc],
                  stat[:, 8 + tc:9 + tc], AF.Copy,
                  scale=stat[:, 16 + tc:17 + tc], waits=[("V", v_rc)],
                  inc="A")
        return a_nm

    def ln_pair(jobs):
        dbs = [dump, dump2]
        a_sq = []
        for (src, tc, w0), db in zip(jobs, dbs):
            a_sq.append(ac("activation", db, src, AF.Square,
                           waits=list(w0) + [("V", ms["v_ssq"])], inc="A"))
        for (src, tc, w0), db in zip(jobs, dbs):
            ve("reduce_sum", stat[:, 4 + tc:5 + tc], src, AX.X,
               waits=list(w0))
        for i, ((src, tc, w0), db) in enumerate(zip(jobs, dbs)):
            ve("reduce_sum", stat[:, tc:tc + 1], db, AX.X,
               waits=[("A", a_sq[i])])
        v_ssq = prog.emit("vector", lambda e: e.drain(), inc="V")
        ms["v_ssq"] = v_ssq
        a_t1 = []
        for (src, tc, w0) in jobs:
            a_t1.append(ac("activation", stat[:, 12 + tc:13 + tc],
                           stat[:, 4 + tc:5 + tc], AF.Square,
                           waits=[("V", v_ssq)], inc="A"))
        for (src, tc, w0) in jobs:
            ve("tensor_scalar", stat[:, 16 + tc:17 + tc],
               stat[:, 4 + tc:5 + tc], -1.0 / D, None, ALU.mult)
        v_var = []
        for i, (src, tc, w0) in enumerate(jobs):
            v_var.append(ve("scalar_tensor_tensor", stat[:, 8 + tc:9 + tc],
                            stat[:, tc:tc + 1], float(D),
                            stat[:, 12 + tc:13 + tc],
                            ALU.mult, ALU.subtract,
                            waits=[("A", a_t1[i])], inc="V"))
        a_std = []
        for i, (src, tc, w0) in enumerate(jobs):
            a_std.append(ac("activation", stat[:, 8 + tc:9 + tc],
                            stat[:, 8 + tc:9 + tc], AF.Sqrt, bias=epsv,
                            scale=1.0 / (D * D), waits=[("V", v_var[i])],
                            inc="A"))
        for i, (src, tc, w0) in enumerate(jobs):
            ve("reciprocal", stat[:, 8 + tc:9 + tc],
               stat[:, 8 + tc:9 + tc], waits=[("A", a_std[i])])
        v_rc = prog.emit("vector", lambda e: e.drain(), inc="V")
        a_nm = []
        for (src, tc, w0) in jobs:
            a_nm.append(ac("activation", stat[:, 12 + tc:13 + tc],
                           stat[:, 8 + tc:9 + tc], AF.Copy,
                           scale=stat[:, 16 + tc:17 + tc],
                           waits=[("V", v_rc)], inc="A"))
        return a_nm

    ln1_state = {}

    def emit_ln1(it2, tc, extra_w=()):
        r0b = it2 * PT
        xv = dma_x(x2[:, tc * D:(tc + 1) * D],
                   x_in[r0b + tc * 128:r0b + (tc + 1) * 128, :],
                   sem="XI")
        a_nm = ln_tc(x2[:, tc * D:(tc + 1) * D], tc, [("XI", xv)])
        lt = V(o_g + tc * 4096, 4096, BF16)
        ln1_state[(it2, tc)] = ve("tensor_scalar", lt,
                                  x2[:, tc * D:(tc + 1) * D],
                                  stat[:, 8 + tc:9 + tc],
                                  stat[:, 12 + tc:13 + tc], ALU.mult,
                                  ALU.add,
                                  waits=[("A", a_nm)] + list(extra_w),
                                  inc="V")

    try:
      for it in range(NTILE):
        r0 = it * PT
        xs = [x2[:, tc * D:(tc + 1) * D] for tc in range(4)]
        if it == 0:
            for pr in range(2):
                tcs = [2 * pr, 2 * pr + 1]
                xvs = {t: dma_x(x2[:, t * D:(t + 1) * D],
                                x_in[t * 128:(t + 1) * 128, :], sem="XI")
                       for t in tcs}
                a_l = ln_pair([(x2[:, t * D:(t + 1) * D], t,
                                [("XI", xvs[t])]) for t in tcs])
                for i, t in enumerate(tcs):
                    lt = V(o_g + t * 4096, 4096, BF16)
                    ln1_state[(0, t)] = ve(
                        "tensor_scalar", lt, x2[:, t * D:(t + 1) * D],
                        stat[:, 8 + t:9 + t], stat[:, 12 + t:13 + t],
                        ALU.mult, ALU.add, waits=[("A", a_l[i])], inc="V")

        ev_bank = [ms["ps67_free_V"], ms["ps67_free_V"]]
        for tc in range(4):
            v_n = ln1_state[(it, tc)]
            lt = V(o_g + tc * 4096, 4096, BF16)
            for dc in range(NDC):
                bank = 6 + dc // 8
                j = dc % 8
                pbf = ps[bank][:, 0:512].bitcast(BF16)
                tr(pbf[:, j * 128:(j + 1) * 128],
                   lt[:, dc * 128:(dc + 1) * 128],
                   waits=([("V", v_n), ("V", ev_bank[dc // 8])]
                          if j == 0 else []),
                   inc="T" if j == 7 else None)
            tw = prog.val("T")
            for half in range(2):
                pbf = ps[6 + half][:, 0:512].bitcast(BF16)
                for j in range(8):
                    dc = half * 8 + j
                    w0 = []
                    if half == 0 and j == 0:
                        w0 = [("T", tw), ("A", ms.get("ffnA", 0))]
                    ve("tensor_copy", sl3(ln1_3, dc, tc * 128, (tc + 1) * 128),
                       pbf[:, j * 128:(j + 1) * 128], waits=w0,
                       inc="V" if j == 7 else None)
                ev_bank[half] = prog.val("V")
        ms["ps67_free_V"] = prog.val("V")

        ve("tensor_tensor", d1_3[:, :, 0:1], carry[:, 0:16].unsqueeze(2),
           ln1_3[:, :, 0:1], ALU.subtract)
        v_d1 = ve("tensor_tensor", d1_3[:, :, 1:512], ln1_3[:, :, 0:511],
                  ln1_3[:, :, 1:512], ALU.subtract, inc="V")
        xxx_v = [0] * NDC
        for dc in range(NDC):
            xxx_v[dc] = ve("scalar_tensor_tensor", sl3(xf_3, dc, 0, 512),
                           sl3(d1_3, dc, 0, 512), maa[:, dc:dc + 1],
                           sl3(ln1_3, dc, 0, 512), ALU.mult, ALU.add,
                           waits=([("V", v_d1), ("XC", X_CONST),
                                   ("WC", W_CONST)]
                                  if dc == 0 else []), inc="V")
        v_xxx = ve("tensor_copy", carry[:, 0:16],
                   ln1_3[:, :, 511:512].squeeze(2), inc="V")

        t_t5 = 0
        for f in range(5):
            for dc in range(NDC):
                w0t = [("V", xxx_v[dc])] if f == 0 else []
                if f == 0 and dc == 0:
                    w0t += [("WC", W_CONST), ("V", ms["ps03_free_V"]),
                            ("A", prog.val("A"))]
                t_t5 = mm(ps[f][0:32, 0:512],
                          w1v[:, dc * 160 + f * 32: dc * 160 + (f + 1) * 32],
                          sl3(xf_3, dc, 0, 512), dc == 0, dc == 15,
                          waits=w0t, inc="T" if dc == 15 else None)
        a_t5 = 0
        for f in range(5):
            a_t5 = ac("activation", t5v[:, f * 512:(f + 1) * 512],
                      ps[f][0:32, 0:512], AF.Tanh,
                      waits=[("T", t_t5)], inc="A")

        qe = fm3(o_q, parts=64, p0=0)
        qo = fm3(o_q, parts=64, p0=64)
        ke = fm3(o_k, parts=64, p0=0)
        ko = fm3(o_k, parts=64, p0=64)

        def kdtm_v(tc):
            return V(o_ext + 8192 + (tc % 2) * 4096, 4096, BF16)

        def enpc_v(tc):
            return V(o_enpc + (tc % 2) * 64, 64, F32)

        def emit_decay2_round(tc, rnd, extra):
            t_vals = []
            for gg in range(2):
                g = 2 * rnd + gg
                t = 0
                for j in range(4):
                    oc = g * 4 + j
                    w0 = list(extra) if (gg == 0 and j == 0) else []
                    t = mm(ps[gg][:, j * 128:(j + 1) * 128],
                           dw2[:, oc * 128:(oc + 1) * 128],
                           t6v[:, tc * 128:(tc + 1) * 128], True, True,
                           waits=w0, inc="T" if j == 3 else None)
                t_vals.append(t)
            return t_vals

        def emit_chain_group(tc, g, t_dec_g):
            c0 = tc * 128
            bank = g % 2
            psg3 = ps[bank][:, 0:512].rearrange("p (a b) -> p a b", a=4)
            tdb = tdv[:, g * 4:(g + 1) * 4].unsqueeze(2) \
                .broadcast_to([128, 4, 128])
            scrB3 = scrB.rearrange("p (a b) -> p a b", a=4)
            v1 = ve("tensor_tensor", scrB3, psg3, tdb, ALU.add,
                    waits=[("T", t_dec_g), ("A", prog.val("A")),
                           ("P", ms["chain_prevP"])], inc="V")
            a1 = ac("activation", scrB, scrB, AF.Exp, waits=[("V", v1)],
                    inc="A")
            for j in range(4):
                dc = g * 4 + j
                ve("tensor_tensor_scan", Pv[:, dc * 128:(dc + 1) * 128],
                   scrB[:, j * 128:(j + 1) * 128],
                   scrB[:, j * 128:(j + 1) * 128], 0.0, ALU.add, ALU.bypass,
                   waits=[("A", a1)] if j == 0 else [])
            v_sc = prog.emit("vector", lambda e: e.drain(), inc="V")
            a_em = ac("activation", scrA, scrB, AF.Exp, scale=-1.0,
                      waits=[("A", a1), ("P", ms["chain_prevP"])], inc="A")
            a_eP = ac("activation", scrC, Pv[:, g * 512:(g + 1) * 512],
                      AF.Exp, waits=[("V", v_sc)], inc="A")
            ve("tensor_tensor", scrA, scrA, scrC, ALU.mult,
               waits=[("A", a_eP)])
            ve("tensor_tensor", scrC, scrC, scrA, ALU.subtract)
            v_kds = prog.emit("vector", lambda e: e.drain(), inc="V")
            scrC3 = scrC.rearrange("p (a b) -> p a b", a=4)
            po("tensor_tensor", k_3[:, g * 4:(g + 1) * 4, c0:c0 + 128],
               k_3[:, g * 4:(g + 1) * 4, c0:c0 + 128], scrC3, ALU.mult,
               waits=[("V", v_kds)])
            a_eN = ac("activation", scrA, Pv[:, g * 512:(g + 1) * 512],
                      AF.Exp, scale=-1.0, waits=[("V", v_kds)], inc="A")
            scrA3 = scrA.rearrange("p (a b) -> p a b", a=4)
            po("tensor_tensor", q_3[:, g * 4:(g + 1) * 4, c0:c0 + 128],
               q_3[:, g * 4:(g + 1) * 4, c0:c0 + 128], scrA3,
               ALU.mult, waits=[("A", a_eN)])
            v_q = prog.emit("gpsimd", lambda e: e.drain(), inc="P")
            ms["chainP"][(tc, g)] = v_q
            ms["chain_prevP"] = v_q
            return v1, v_q

        def emit_kdT(tc, v_scales):
            c0 = tc * 128
            t_kt = 0
            for dc in range(NDC):
                bank = dc // 8
                j = dc % 8
                pbf = ps[bank][:, 0:512].bitcast(BF16)
                t_kt = tr(pbf[:, j * 128:(j + 1) * 128],
                          sl3(k_3, dc, c0, c0 + 128),
                          waits=[("P", v_scales)] if dc == 0 else [],
                          inc="T" if j == 7 else None)
            return t_kt

        def emit_kdT_A(tc, t_kt):
            kv = kdtm_v(tc)
            vv = 0
            for half in range(2):
                pbf = ps[half][:, 0:512].bitcast(BF16)
                for j in range(8):
                    dc = half * 8 + j
                    vv = ac("activation", kv[:, dc * 128:(dc + 1) * 128],
                            pbf[:, j * 128:(j + 1) * 128], AF.Copy,
                            waits=[("T", t_kt)]
                            if (half == 0 and j == 0) else [],
                            inc="A" if (half == 1 and j == 7) else None)
            return vv

        att_pro = []

        def exec_prologue():
            td0 = emit_decay2_round(0, 0, [("A", ms["t6_A"]),
                                           ("V", ms["mix_stt"][0]),
                                           ("V", ms["mix_stt"][1])])
            v1_a, _ = emit_chain_group(0, 0, td0[0])
            v1_b, _ = emit_chain_group(0, 1, td0[1])
            td1 = emit_decay2_round(0, 1, [("V", v1_a), ("V", v1_b)])
            emit_chain_group(0, 2, td1[0])
            _, vq0 = emit_chain_group(0, 3, td1[1])
            a_pc = {0: ac("activation", enpc_v(0)[:, 0:16], Pv[:, 127::128],
                              AF.Exp, scale=-1.0, waits=[("P", vq0)], inc="A")}
            v_ktm = {0: emit_kdT_A(0, emit_kdT(0, vq0))}


            ms["pro"] = (a_pc, v_ktm)

        PROJ = [("r", "wq"), ("k", "wk"), ("v", "wv"), ("w", None),
                ("g", "wg")]
        def w2chunk(fi2, rnd):
            dst = V(o_w2b + rnd * 2048, 2048, BF16, parts=32)
            return dma_w(dst, w2_in[:, fi2 * D + rnd * 1024:
                                    fi2 * D + (rnd + 1) * 1024],
                         waits=[("T", w2_last_T[rnd])],
                         sem=("W2" if rnd == 0 else "W9"))

        w2v = [w2chunk(0, 0), w2chunk(0, 1)]
        for fi, (fname, wname) in enumerate(PROJ):
            fidx = fi + 1
            mix_V = [0] * NDC
            v_round = prog.val("V")
            mixt8 = V(o_lt, 8192, BF16)
            mixt83 = mixt8.rearrange("p (a b) -> p a b", a=8)
            for rnd in range(2):
                t_m = 0
                for j in range(8):
                    dc = rnd * 8 + j
                    if j == 0:
                        w0 = [("W2" if rnd == 0 else "W9", w2v[rnd]),
                              ("A", prog.val("A"))]
                        if rnd == 0:
                            w0.append(("V", v_round))
                    else:
                        w0 = []
                    if rnd == 1:
                        w0.append(("V", ms["mix_stt"][j]))
                    t_m = mm(ps[j][:, 0:512],
                             w2b[:, dc * 128:(dc + 1) * 128],
                             t5v[:, fi * 512:(fi + 1) * 512], True, True,
                             waits=w0, inc="T" if j == 7 else None)
                w2_last_T[rnd] = t_m
                for j in range(8):
                    dc = rnd * 8 + j
                    ms["mix_stt"][j] = ve(
                        "scalar_tensor_tensor",
                        mixt8[:, j * 512:(j + 1) * 512],
                        ps[j][:, 0:512],
                        maa[:, fidx * 16 + dc:fidx * 16 + dc + 1],
                        sl3(d1_3, dc, 0, 512), ALU.add, ALU.mult,
                        waits=[("T", t_m)] if j == 0 else [], inc="V")
                for hf in range(2):
                    j0 = rnd * 8 + hf * 4
                    v_round2 = ve("tensor_tensor",
                                  xf_3[:, j0:j0 + 4, :],
                                  mixt83[:, hf * 4:hf * 4 + 4],
                                  ln1_3[:, j0:j0 + 4, :], ALU.add,
                                  inc="V")
                    for j in range(4):
                        mix_V[j0 + j] = v_round2
            if fi + 1 < len(PROJ):
                w2v = [w2chunk(fi + 1, 0), w2chunk(fi + 1, 1)]
            mixes_done = prog.val("V")

            if fname == "w":
                t_t6 = 0
                for dc in range(NDC):
                    t_t6 = mm(ps[7][0:64, 0:512], dw1[:, dc * 64:(dc + 1) * 64],
                              sl3(xf_3, dc, 0, 512), dc == 0, dc == 15,
                              waits=([("V", mixes_done),
                                      ("A", prog.val("A"))]
                                     if dc == 0 else []),
                              inc="T" if dc == 15 else None)
                ms["t6_A"] = ac("activation", t6v, ps[7][0:64, 0:512], AF.Tanh,
                                waits=[("T", t_t6)], inc="A")
                continue

            if fname == "v":
                ev = 0
                for half in range(2):
                    for dc in range(NDC):
                        sl = (half * 16 + dc) % 4
                        wbv = V(ws_off[sl], 2048, BF16)
                        w0d = [("T", ws_last_T[sl]),
                               ("WB", ws_wb_slot[sl])]
                        if it == 0:
                            wW2 = dma_w(wbv, w_proj["wv"][half, dc],
                                        waits=w0d, sem=WS_SEMS[sl])
                        else:
                            w0d.append(("WB", wo_wb[("wv", half, dc)]))
                            wW2 = dma_w(wbv, w_scr[2, half, dc],
                                        waits=w0d, sem=WS_SEMS[sl])
                        for tc in range(4):
                            for og in range(2):
                                w0 = []
                                if tc == 0 and og == 0:
                                    w0 = [(WS_SEMS[sl], wW2),
                                          ("V", mix_V[dc]), ("A", ev)]
                                    if half == 0 and dc == 0:
                                        w0 += [("V", mixes_done),
                                               ("A", prog.val("A"))]
                                mm(ps[tc * 2 + og][:, 0:512],
                                   sl3(xf_3, dc, tc * 128,
                                       (tc + 1) * 128),
                                   wbv[:, og * 512:(og + 1) * 512],
                                   dc == 0, dc == 15, waits=w0,
                                   inc=("T" if (tc == 3 and og == 1)
                                        else None))
                        ws_last_T[sl] = prog.val("T")
                        if it == 0:
                            wo_wb[("wv", half, dc)] = dma_wb(
                                w_scr[2, half, dc], wbv,
                                waits=[("T", ws_last_T[sl])])
                            ws_wb_slot[sl] = wo_wb[("wv", half, dc)]
                    tw = prog.val("T")
                    for tc in range(4):
                        for og in range(2):
                            o_col = tc * D + (half * 2 + og) * 512
                            ev2 = ac("activation",
                                     vtm[:, o_col:o_col + 512],
                                     ps[tc * 2 + og][:, 0:512], AF.Copy,
                                     waits=[("T", tw)]
                                     if (tc == 0 and og == 0) else [],
                                     inc=("A" if (tc == 3 and og == 1)
                                          else None))
                            if tc == 3 and og == 1:
                                ev = ev2
                continue

            if fname == "g":
                exec_prologue()
            proj_gate_V = prog.val("V")
            ev = 0
            eva = 0
            for half in range(2):
                for dc in range(NDC):
                    sl = (half * 16 + dc) % 4
                    wbv = V(ws_off[sl], 2048, BF16)
                    w0d = [("T", ws_last_T[sl]),
                           ("WB", ws_wb_slot[sl])]
                    if it == 0:
                        wW2 = dma_w(wbv, w_proj[wname][half, dc],
                                    waits=w0d, sem=WS_SEMS[sl])
                    else:
                        w0d.append(("WB", wo_wb[(wname, half, dc)]))
                        wW2 = dma_w(wbv, w_scr[scr_ix[wname], half, dc],
                                    waits=w0d, sem=WS_SEMS[sl])
                    for oc in range(8):
                        w0 = []
                        if oc == 0:
                            w0 = [(WS_SEMS[sl], wW2), ("V", mix_V[dc]),
                                  ("A", ev), ("A", eva)]
                            if half == 0 and dc == 0:
                                w0 += [("V", proj_gate_V),
                                       ("A", prog.val("A"))]
                        mm(ps[oc][:, 0:512],
                           wbv[:, oc * 128:(oc + 1) * 128],
                           sl3(xf_3, dc, 0, 512),
                           dc == 0, dc == 15, waits=w0,
                           inc=("T" if oc == 7 else None))
                    ws_last_T[sl] = prog.val("T")
                    if it == 0:
                        wo_wb[(wname, half, dc)] = dma_wb(
                            w_scr[scr_ix[wname], half, dc], wbv,
                            waits=[("T", ws_last_T[sl])])
                        ws_wb_slot[sl] = wo_wb[(wname, half, dc)]
                tw = prog.val("T")
                for oc in range(8):
                    ocg = half * 8 + oc
                    if fname == "g":
                        eva = ac("activation", sl3(g_3, ocg, 0, 512),
                                 ps[oc][:, 0:512], AF.Silu,
                                 bias=bgv[:, ocg:ocg + 1],
                                 waits=[("T", tw)] if oc == 0 else [],
                                 inc="A" if oc == 7 else None)
                    else:
                        dst3 = q_3 if fname == "r" else k_3
                        ev2 = ac("activation", sl3(dst3, ocg, 0, 512),
                                 ps[oc][:, 0:512], AF.Copy,
                                 waits=[("T", tw)] if oc == 0 else [],
                                 inc="A" if oc == 7 else None)
                        if oc == 7:
                            ev = ev2

        PROJ_DONE_V = prog.val("V")
        PROJ_DONE_A = prog.val("A")
        PROJ_DONE_T = prog.val("T")
        for _i in range(4, 8):
            ws_last_T[_i] = PROJ_DONE_T
        if it == 0:
            stop_here("pg1", [
                ("stat", stat), ("x2d", x2),
                ("ln1", V(o_ext, 16384, BF16)), ("d1", V(o_d1, 16384, BF16)),
                ("xg", V(o_xf, 16384, BF16)), ("q", V(o_q, 16384, BF16)),
                ("k", V(o_k, 16384, BF16)), ("v", vtm),
                ("g", V(o_g, 16384, BF16)), ("t6", t6v), ("t5", t5v)])

        a_pc, v_ktm = ms["pro"]

        atm_free = [prog.val("V")] * 2
        yg_free = [prog.val("A")] * 2
        supd_T = [0] * 4
        supd_v = list(ms.get("supd_groups", [0, 0, 0, 0]))
        sbf_prev = ms["sbf_V"]

        def emit_AT(tc, p):
            c0 = tc * 128
            t_at = 0
            psb = ps[4 + p % 2][:, 0:512]
            for idx, dc in ((0, 2 * p), (1, 2 * p + 1)):
                off = idx * 256
                w0 = []
                if idx == 0:
                    w0 = [("V", atm_free[p % 2]),
                          ("P", ms["chainP"][(tc, p // 2)])]
                    if p == 0 and tc == 0:
                        w0 += [("A", v_ktm[0]), ("A", PROJ_DONE_A)]
                mm(psb[:, off:off + 128],
                   sl3(ke, dc, c0, c0 + 128), sl3(qe, dc, c0, c0 + 128),
                   True, True, waits=w0, tp=(0, 0))
                mm(psb[0:64, off + 128:off + 256],
                   sl3(ko, dc, c0, c0 + 128)[:, 0:64],
                   sl3(qo, dc, c0, c0 + 128), True, True, tp=(64, 0))
                t_at = mm(psb[64:128, off + 128:off + 256],
                          sl3(ko, dc, c0, c0 + 128)[:, 64:128],
                          sl3(qo, dc, c0, c0 + 128), True, True,
                          tp=(64, 64), inc="T" if idx == 1 else None)
            return t_at

        pairs = [(tc, p) for tc in range(4) for p in range(8)]
        at_T = {pairs[0]: emit_AT(*pairs[0])}
        nxt = {}
        for i, (tc, p) in enumerate(pairs):
            c0 = tc * 128
            kv = kdtm_v(tc)
            dc0, dc1 = 2 * p, 2 * p + 1
            if i + 1 < len(pairs):
                at_T[pairs[i + 1]] = emit_AT(*pairs[i + 1])
            psb = ps[4 + p % 2][:, 0:512]
            t_at = at_T[(tc, p)]
            if tc < 3 and p == 0:
                nxt = {}
                nxt["td0"] = emit_decay2_round(tc + 1, 0,
                                               [("A", v_ktm[tc])])
            if tc < 3 and p == 2:
                nxt["td1"] = emit_decay2_round(
                    tc + 1, 1, [("V", nxt["v1_0"]), ("V", nxt["v1_1"])])
            if tc < 3 and p == 5:
                nxt["t_kt"] = emit_kdT(tc + 1, nxt["vq"])
            for idx, dc in ((0, dc0), (1, dc1)):
                he = 2 * dc
                g = dc // 4
                bS = 2 + g // 2
                base = (g % 2) * 256 + (dc % 4) * 64
                w0s = []
                if dc % 4 == 0:
                    w0s = [("V", supd_v[g]), ("A", v_ktm[tc])]
                mm(ps[bS][0:64, base:base + 64],
                   kv[:, dc * 128:dc * 128 + 64],
                   vtm[:, tc * D + he * 64: tc * D + (he + 1) * 64],
                   True, True, waits=w0s, tp=(0, 0))
                supd_T[g] = mm(ps[bS][64:128, base:base + 64],
                               kv[:, dc * 128 + 64:dc * 128 + 128],
                               vtm[:, tc * D + (he + 1) * 64:
                                   tc * D + (he + 2) * 64],
                               True, True, tp=(0, 64),
                               inc="T" if dc % 4 == 3 else None)
            atmv = V(o_dump + (p % 2) * 1024, 1024, BF16)
            atm3 = atmv.rearrange("p (a b) -> p a b", a=4)
            ps3a = psb.rearrange("p (a b) -> p a b", a=4)
            mk3 = maskm.unsqueeze(1).broadcast_to([128, 2, 128])
            hs = []
            for idx in range(2):
                hs.append(ve("tensor_tensor", atm3[:, 2 * idx:2 * idx + 2],
                             ps3a[:, 2 * idx:2 * idx + 2], mk3, ALU.mult,
                             waits=[("T", t_at)] if idx == 0 else [],
                             inc="V"))
            atm_free[p % 2] = hs[1]
            bY = 6 + p % 2
            t_y = 0
            for idx, dc in ((0, dc0), (1, dc1)):
                off = idx * 128
                he = 2 * dc
                for rw in range(2):
                    r0_, r1_ = 64 * rw, 64 * (rw + 1)
                    w0 = []
                    if rw == 0 and idx == 0:
                        w0.append(("A", yg_free[p % 2]))
                        if p == 0:
                            w0.append(("V", sbf_prev))
                    mm(ps[bY][r0_:r1_, off:off + 128],
                       Sb[r0_:r1_, dc * 64:(dc + 1) * 64],
                       sl3(qe if rw == 0 else qo, dc, c0, c0 + 128),
                       True, False, waits=w0,
                       tp=(0, 0) if rw == 0 else (64, 64))
                    t_y = mm(ps[bY][r0_:r1_, off:off + 128],
                             vtm[:, tc * D + (he + rw) * 64:
                                 tc * D + (he + rw + 1) * 64],
                             atmv[:, idx * 256 + rw * 128:
                                  idx * 256 + rw * 128 + 128],
                             False, True, waits=[("V", hs[idx])],
                             tp=(0, 0) if rw == 0 else (0, 64),
                             inc="T" if (rw == 1 and idx == 1) else None)
            ystage = V(o_dump + 2048 + (p % 2) * 512, 512, BF16)
            yst3 = ystage.rearrange("p (a b) -> p a b", a=2)
            g3sl = g_3[:, dc0:dc0 + 2, c0:c0 + 128]
            yg_free[p % 2] = ac("activation", ystage,
                                ps[bY][:, 0:256].rearrange(
                                    "p (a b) -> p a b", a=2),
                                AF.Copy, waits=[("T", t_y)], inc="A")
            po("tensor_tensor", g3sl, yst3, g3sl, ALU.mult,
               waits=[("A", yg_free[p % 2])])
            if p == 7:
                ms["ygP"] = prog.emit("gpsimd", lambda e: e.drain(),
                                      inc="P")
            if tc < 3:
                if p == 0:
                    nxt["v1_0"], _ = emit_chain_group(tc + 1, 0,
                                                      nxt["td0"][0])
                elif p == 1:
                    nxt["v1_1"], _ = emit_chain_group(tc + 1, 1,
                                                      nxt["td0"][1])
                elif p == 2:
                    emit_chain_group(tc + 1, 2, nxt["td1"][0])
                elif p == 3:
                    _, nxt["vq"] = emit_chain_group(tc + 1, 3,
                                                    nxt["td1"][1])
                    a_pc[tc + 1] = ac("activation",
                                      enpc_v(tc + 1)[:, 0:16],
                                      Pv[:, 127::128], AF.Exp,
                                      scale=-1.0,
                                      waits=[("P", nxt["vq"])], inc="A")
            if p % 2 == 1:
                g = p // 2
                bS = 2 + g // 2
                gb = (g % 2) * 256
                s_sl = Sv[:, g * 256:(g + 1) * 256]
                ve("tensor_tensor", s_sl, s_sl,
                   ps[bS][:, gb:gb + 256], ALU.add,
                   waits=[("T", supd_T[g]), ("T", t_y),
                          ("A", a_pc[tc])])
                s3 = s_sl.rearrange("p (a b) -> p a b", a=4)
                eb = enpc_v(tc)[:, g * 4:(g + 1) * 4] \
                    .unsqueeze(2).broadcast_to([128, 4, 64])
                sb3 = Sb[:, g * 256:(g + 1) * 256] \
                    .rearrange("p (a b) -> p a b", a=4)
                ve("tensor_tensor", sb3, s3, eb, ALU.mult)
                ve("tensor_tensor", s3, s3, eb, ALU.mult)
                supd_v[g] = prog.emit("vector", lambda e: e.drain(),
                                      inc="V")
            if p == 7:
                if tc < 3:
                    v_ktm[tc + 1] = emit_kdT_A(tc + 1, nxt["t_kt"])
                sbf_prev = prog.val("V")
                if tc == 1:
                    ms["ygP_tc1"] = ms["ygP"]
        ms["sbf_V"] = prog.val("V")
        ms["supd_groups"] = list(supd_v)
        ms["ps67_free2_V"] = prog.val("V")
        YG_V = prog.val("V")
        ms["ps03_free_V"] = YG_V
        if it == 0:
            stop_here("pg2", [("yg", V(o_g, 16384, BF16))])

        ev = 0
        v_res = 0
        ev_bank = [ms["ps67_free2_V"], ms["ps67_free2_V"]]
        ln2_done = [0, 0, 0, 0]
        wo_rel = [("A", v_ktm[3]), ("A", v_ktm[3]),
                  ("V", supd_v[1]), ("V", supd_v[3]),
                  ("V", atm_free[0]), ("V", atm_free[1]),
                  ("A", yg_free[0]), ("A", yg_free[1])]
        for pas in range(2):
            for dc in range(NDC):
                for half in range(2):
                    sl = (pas * 32 + dc * 2 + half) % 8
                    wbv = V(ws_off[sl], 2048, BF16)
                    w0d = [("T", ws_last_T[sl]),
                           ("WB", ws_wb_slot[sl])]
                    if it == 0 and pas == 0:
                        wWx = dma_w(wbv, w_proj["wo"][half, dc],
                                    waits=w0d, sem=WS_SEMS[sl])
                    else:
                        w0d.append(("WB", wo_wb[(half, dc)]))
                        wWx = dma_w(wbv, wo_scr[half, dc],
                                    waits=w0d, sem=WS_SEMS[sl])
                    for tcl in range(2):
                        tc = 2 * pas + tcl
                        for og in range(2):
                            bank = tcl * 4 + half * 2 + og
                            w0 = []
                            if tcl == 0 and og == 0:
                                w0 = [(WS_SEMS[sl], wWx)]
                            if dc == 0:
                                w0.append(wo_rel[bank] if pas == 0
                                          else ("V", ev))
                                if half == 0 and tcl == 0 and og == 0:
                                    w0.append(("P", ms["ygP"] if pas == 1
                                               else ms["ygP_tc1"]))
                                if pas == 0 and half == 0 and tcl == 0 \
                                        and og == 0:
                                    w0.append(("A", prog.val("A")))
                            mm(ps[bank][:, 0:512],
                               sl3(g_3, dc, tc * 128, (tc + 1) * 128),
                               wbv[:, og * 512:(og + 1) * 512],
                               dc == 0, dc == 15, waits=w0,
                               inc=("T" if (tcl == 1 and og == 1)
                                    else None))
                    ws_last_T[sl] = prog.val("T")
                    if it == 0 and pas == 0:
                        wo_wb[(half, dc)] = dma_wb(
                            wo_scr[half, dc], wbv,
                            waits=[("T", ws_last_T[sl])])
                        ws_wb_slot[sl] = wo_wb[(half, dc)]
            tw = prog.val("T")
            for tcl in range(2):
                tc = 2 * pas + tcl
                for half in range(2):
                    for og in range(2):
                        bank = tcl * 4 + half * 2 + og
                        o_col = tc * D + (half * 2 + og) * 512
                        ev = ve("tensor_copy", attn_tm[:, o_col:o_col + 512],
                                ps[bank][:, 0:512],
                                waits=[("T", tw)]
                                if (tcl == 0 and half == 0 and og == 0)
                                else [],
                                inc=("V" if (tcl == 1 and half == 1
                                             and og == 1) else None))
            if pas == 1:
                ms["ps67_free2_V"] = ev
            chain_tc = [2 * pas, 2 * pas + 1]
            a_at = ln_pair([(attn_tm[:, t * D:(t + 1) * D], t,
                             [("V", ev)] + ([("P", ms["ygP"])] if i2 == 0
                                            else []))
                            for i2, t in enumerate(chain_tc)])
            for i, t in enumerate(chain_tc):
                at_s = attn_tm[:, t * D:(t + 1) * D]
                ve("tensor_scalar", at_s, at_s, stat[:, 8 + t:9 + t],
                   stat[:, 12 + t:13 + t], ALU.mult, ALU.add,
                   waits=[("A", a_at[i])])
                ve("tensor_tensor", xs[t], xs[t], at_s, ALU.add)
            v_res = prog.emit("vector", lambda e: e.drain(), inc="V")
            a_l2 = ln_pair([(xs[t], t, [("V", v_res)]) for t in chain_tc])
            for i, t in enumerate(chain_tc):
                lt = V(o_d1 + t * 4096, 4096, BF16)
                ln2_done[t] = ve("tensor_scalar", lt, xs[t],
                                 stat[:, 8 + t:9 + t],
                                 stat[:, 12 + t:13 + t],
                                 ALU.mult, ALU.add,
                                 waits=[("A", a_l2[i])], inc="V")
        if it == 0:
            stop_here("lnattn", [("x2", x2)])
        for tc in range(4):
            lt = V(o_d1 + tc * 4096, 4096, BF16)
            for dc in range(NDC):
                bank = 6 + dc // 8
                j = dc % 8
                pbf = ps[bank][:, 0:512].bitcast(BF16)
                tr(pbf[:, j * 128:(j + 1) * 128],
                   lt[:, dc * 128:(dc + 1) * 128],
                   waits=([("V", ln2_done[tc]), ("V", ev_bank[dc // 8]),
                           ("V", ev)] if j == 0 else []),
                   inc="T" if j == 7 else None)
            tw = prog.val("T")
            for half in range(2):
                pbf = ps[6 + half][:, 0:512].bitcast(BF16)
                for j in range(8):
                    dc = half * 8 + j
                    ve("tensor_copy", sl3(ln2_3, dc, tc * 128, (tc + 1) * 128),
                       pbf[:, j * 128:(j + 1) * 128],
                       waits=[("T", tw)] if (half == 0 and j == 0) else [],
                       inc="V" if j == 7 else None)
                ev_bank[half] = prog.val("V")
        LN2T_T = prog.val("T")
        LN2_V = prog.val("V")
        if it == 0:
            stop_here("ln2", [("ln2", V(o_g, 16384, BF16))])

        ev_pair = [prog.val("A"), prog.val("A")]
        for ic in range(NIC):
            r = ic % 2
            gwv = V(o_gwb[r], 4096, BF16)
            uwv = V(o_uwb[r], 4096, BF16)
            dwait = [("T", gwb_last_T[r])]
            if ic < 2:
                dwait.append(("T", LN2T_T))
            wWg = dma_w(gwv, wg8_in[ic], waits=dwait, sem=f"G{r}")
            dwait = [("T", uwb_last_T[r])]
            if ic < 2:
                dwait.append(("T", LN2T_T))
            wWu = dma_w(uwv, wu8_in[ic], waits=dwait, sem=f"U{r}")
            bG, bU = 2 * r, 2 * r + 1
            t_g = 0
            for dc in range(NDC):
                t_g = mm(ps[bG][:, 0:512], gwv[:, dc * 128:(dc + 1) * 128],
                         sl3(ln2_3, dc, 0, 512), dc == 0, dc == 15,
                         waits=([(f"G{r}", wWg), ("V", LN2_V),
                                 ("A", ev_pair[r])] if dc == 0 else []),
                         inc="T" if dc == 15 else None)
            gwb_last_T[r] = prog.val("T")
            t_u = 0
            for dc in range(NDC):
                t_u = mm(ps[bU][:, 0:512], uwv[:, dc * 128:(dc + 1) * 128],
                         sl3(ln2_3, dc, 0, 512), dc == 0, dc == 15,
                         waits=[(f"U{r}", wWu)] if dc == 0 else [],
                         inc="T" if dc == 15 else None)
            uwb_last_T[r] = prog.val("T")
            sgb = V(o_sgb[r], 1024, BF16)
            a_si = ac("activation", sgb, ps[bG][:, 0:512], AF.Silu,
                      scale=sgv[:, ic:ic + 1], waits=[("T", t_g)], inc="A")
            ev_pair[r] = a_si
            ve("scalar_tensor_tensor", fuse_ic(ic), ps[bU][:, 0:512],
               suv[:, ic:ic + 1], sgb, ALU.mult, ALU.mult,
               waits=[("T", t_u), ("A", a_si)], inc="V")
        prog.emit("vector", lambda e: e.drain(), inc="V")
        FUSE_V = prog.val("V")
        FUSE_A = prog.val("A")
        if it == 0:
            stop_here("fuse", [
                ("fuse0", V(o_xf, 16384, BF16)), ("fuse1", V(o_q, 16384, BF16)),
                ("fuse2", V(o_k, 16384, BF16)), ("fuse3", V(o_v, 16384, BF16))])

        ev = 0
        dn_base_T = prog.val("T")
        wd_first = [[("V", PROJ_DONE_V)], [("V", PROJ_DONE_V)],
                    [("T", gwb_last_T[0])], [("T", gwb_last_T[1])],
                    [("T", uwb_last_T[0])], [("T", uwb_last_T[1])]]
        for pas in range(2):
            for ic in range(NIC):
                sl = (pas * NIC + ic) % 6
                wdv = V(wd_off[sl], 4096, BF16)
                if wd_first[sl] is not None:
                    w0d = wd_first[sl]
                    wd_first[sl] = None
                else:
                    w0d = [("T", wd_last_T[sl])]
                wWd = dma_w(wdv, wd8_in[ic * 128:(ic + 1) * 128, :],
                            waits=w0d, sem=WD_SEMS[sl])
                f3 = fuse_ic(ic).rearrange("p (a b) -> p a b", a=4)
                for tcl in range(2):
                    tc = 2 * pas + tcl
                    for oq in range(4):
                        bank = tcl * 4 + oq
                        w0 = []
                        if tcl == 0 and oq == 0:
                            w0 = [(WD_SEMS[sl], wWd)]
                        if ic == 0:
                            w0.append(("V", FUSE_V if pas == 0 else ev))
                            if pas == 0 and tcl == 0 and oq == 0:
                                w0.append(("A", FUSE_A))
                        mm(ps[bank][:, 0:512], sl3(f3, tc, 0, 128),
                           wdv[:, oq * 512:(oq + 1) * 512],
                           ic == 0, ic == NIC - 1, waits=w0,
                           inc=("T" if (tcl == 1 and oq == 3) else None))
                wd_last_T[sl] = prog.val("T")
            tw = prog.val("T")
            for tcl in range(2):
                tc = 2 * pas + tcl
                for oq in range(4):
                    bank = tcl * 4 + oq
                    col = tc * D + oq * 512
                    ev = ve("tensor_tensor", ffn_tm[:, col:col + 512],
                            ps[bank][:, 0:512],
                            sdb[:, oq * 512:(oq + 1) * 512], ALU.mult,
                            waits=[("T", tw)] if (tcl == 0 and oq == 0)
                            else [],
                            inc="V" if (tcl == 1 and oq == 3) else None)
            if pas == 1:
                ms["ps67_free_V"] = ev
            chain_tc = [2 * pas, 2 * pas + 1]
            a_f = ln_pair([(ffn_tm[:, t * D:(t + 1) * D], t, [("V", ev)])
                           for t in chain_tc])
            xvs = {}
            for i, t in enumerate(chain_tc):
                ff_s = ffn_tm[:, t * D:(t + 1) * D]
                ve("tensor_scalar", ff_s, ff_s, stat[:, 8 + t:9 + t],
                   stat[:, 12 + t:13 + t], ALU.mult, ALU.add,
                   waits=[("A", a_f[i])])
                ve("tensor_tensor", xs[t], xs[t], ff_s, ALU.add)
                v_add = prog.emit("vector", lambda e: e.drain(), inc="V")
                dma_st(out_hbm[r0 + t * 128:r0 + (t + 1) * 128, :], xs[t],
                       waits=[("V", v_add)])
                n_stores += 1
                if it + 1 < NTILE:
                    r0b = (it + 1) * PT
                    xvs[t] = dma_x(x2[:, t * D:(t + 1) * D],
                                   x_in[r0b + t * 128:r0b + (t + 1) * 128,
                                        :], sem="XI")
            if it + 1 < NTILE:
                a_l = ln_pair([(x2[:, t * D:(t + 1) * D], t,
                                [("XI", xvs[t])]) for t in chain_tc])
                for i, t in enumerate(chain_tc):
                    lt = V(o_g + t * 4096, 4096, BF16)
                    ln1_state[(it + 1, t)] = ve(
                        "tensor_scalar", lt, x2[:, t * D:(t + 1) * D],
                        stat[:, 8 + t:9 + t], stat[:, 12 + t:13 + t],
                        ALU.mult, ALU.add,
                        waits=[("A", a_l[i]), ("T", dn_base_T)], inc="V")
        DOWN_V = prog.val("V")
        if it == 0:
            stop_here("down", [("ffn", ffn_tm)])
        ms["ps03_free_V"] = ms["ps67_free_V"]
        ms["supd_free_V"] = DOWN_V
        ms["ps67_free2_V"] = DOWN_V
        ms["ffnA"] = prog.val("A")

    except _StopEmit:
        pass

    n_dbg = 0
    for nm, ap in dbg_specs:
        shp = list(ap.shape)
        flat = [shp[0], 1]
        for d in shp[1:]:
            flat[1] *= d
        o = nc.dram_tensor("dbg_" + nm, flat, F32,
                           kind="ExternalOutput").ap()
        wts = [("T", prog.val("T")), ("V", prog.val("V")),
               ("A", prog.val("A"))]

        def fdb(e, o=o, src=ap):
            return e.dma_start(out=o, in_=src)
        prog.emit("gpsimd", fdb, wts, "ST", 16)
        n_dbg += 1
    prog.emit("sync", lambda e: e.nop(),
              waits=[("ST", 16 * (n_stores + n_dbg))])

    blk = enter(nc.Block())
    prog.install(blk)
    for g in reversed(ctx):
        g.__exit__(None, None, None)
    prog.close()
    nc._prog_ops = prog.ops
    return nc


_NC_CACHE = None


def _get_nc():
    global _NC_CACHE
    if _NC_CACHE is None:
        _NC_CACHE = build()
    return _NC_CACHE


def _layout_weights(inp):
    c = np.ascontiguousarray
    dev = {}
    for nm, key in (("wq", "L0_attn_Wq"), ("wk", "L0_attn_Wk"),
                    ("wv", "L0_attn_Wv"), ("wg", "L0_attn_Wg"),
                    ("wo", "L0_attn_Wo")):
        W = np.asarray(inp[key])
        dev[nm] = c(W.T.reshape(NDC, 128, 2, 1024).transpose(2, 0, 1, 3))
    dev["w1"] = c(np.asarray(inp["L0_attn_maa_w1"])
                  .reshape(NDC, 128, 160).transpose(1, 0, 2)
                  .reshape(128, NDC * 160))
    dev["w2"] = c(np.asarray(inp["L0_attn_maa_w2"])
                  .transpose(1, 0, 2).reshape(32, 5 * D))
    dev["dw1"] = c(np.asarray(inp["L0_attn_decay_w1"])
                   .reshape(NDC, 128, 64).transpose(1, 0, 2)
                   .reshape(128, NDC * 64))
    dev["dw2"] = c(np.asarray(inp["L0_attn_decay_w2"]))
    maas = np.stack([np.asarray(inp[f"L0_attn_maa_{f}"]).reshape(D)
                     for f in "xrkvwg"])
    dev["maa6"] = c(maas.reshape(6, NDC, 128).transpose(2, 0, 1)
                    .reshape(128, 96))
    dev["td"] = c(np.asarray(inp["L0_attn_time_decay"]).reshape(NDC, 128).T)
    dev["bg"] = c(np.asarray(inp["L0_attn_bg"]).reshape(NDC, 128).T)
    dev["sg"] = c(np.asarray(inp["L0_ffn_s_gate"]).reshape(NIC, 128).T)
    dev["su"] = c(np.asarray(inp["L0_ffn_s_up"]).reshape(NIC, 128).T)
    dev["sd"] = c(np.asarray(inp["L0_ffn_s_down"]).reshape(1, D))
    for nm, key in (("wg8", "L0_ffn_w_gate"), ("wu8", "L0_ffn_w_up")):
        W = np.asarray(inp[key])
        dev[nm] = c(W.reshape(NDC, 128, NIC, 128).transpose(2, 1, 0, 3)
                    .reshape(NIC, 128, NDC * 128))
    dev["wd8"] = c(np.asarray(inp["L0_ffn_w_down"]))
    dev["idm"] = np.eye(128, dtype=np.float32)
    dev["maskm"] = np.triu(np.ones((128, 128), dtype=np.float32))
    return dev


def kernel(**inputs):
    nc = _get_nc()
    dev = _layout_weights(inputs)
    xall = np.asarray(inputs["L0_attn_hidden_states"])
    in_maps = []
    for b in range(8):
        m = dict(dev)
        m["x"] = np.ascontiguousarray(xall[b])
        in_maps.append(m)
    res = run_bass_kernel_spmd(nc, in_maps, core_ids=list(range(8)))
    return np.stack([res.results[b]["out"] for b in range(8)]).astype(np.float32)


# version 19
# speedup vs baseline: 1.0330x; 1.0330x over previous
import os
import numpy as np
import ml_dtypes
import concourse.bass as bass
import concourse.mybir as mybir
from concourse.bass_utils import run_bass_kernel_spmd

F32 = mybir.dt.float32
BF16 = mybir.dt.bfloat16
I8 = mybir.dt.int8
AF = mybir.ActivationFunctionType
ALU = mybir.AluOpType
AX = mybir.AxisListType

D = 2048
H, HD = 32, 64
NDC = 16
PT = 512
NTILE = 4
T = 2048
NIC = 64
EPS = 1e-5
ARENA = 212832

ENGINES = ("tensor", "vector", "scalar", "gpsimd", "sync")


class _StopEmit(Exception):
    pass


class Prog:
    def __init__(self, nc):
        self.nc = nc
        self.ops = {e: [] for e in ENGINES}
        self.c = {}
        self.sem_handles = {}
        self._g = []

    def make_sems(self, names):
        for n in names:
            g = self.nc.semaphore(n)
            self._g.append(g)
            self.sem_handles[n] = g.__enter__()
            self.c[n] = 0

    def emit(self, eng, fn, waits=(), inc=None, amt=1):
        w = [(s, v) for (s, v) in waits if v > 0]
        import os
        if os.environ.get("KTRACE"):
            import traceback
            frs = traceback.extract_stack(limit=10)
            lines = [f.lineno for f in frs if "kernel" in (f.filename or "")
                     and f.lineno > 300]
            self.ops[eng].append((fn, w, inc, amt, str(lines[-3:])))
        else:
            self.ops[eng].append((fn, w, inc, amt))
        if inc is not None:
            self.c[inc] += amt
        return self.c.get(inc)

    def val(self, sem):
        return self.c[sem]

    def install(self, block):
        sems = self.sem_handles

        def runner(name):
            def _f(eng):
                for fn, waits, inc, amt, *_ in self.ops[name]:
                    for s, v in waits:
                        eng.wait_ge(sems[s], v)
                    inst = fn(eng)
                    if inc is not None:
                        inst.then_inc(sems[inc], amt)
            return _f
        block.tensor(runner("tensor"))
        block.vector(runner("vector"))
        block.scalar(runner("scalar"))
        block.gpsimd(runner("gpsimd"))
        block.sync(runner("sync"))

    def close(self):
        for g in reversed(self._g):
            g.__exit__(None, None, None)


def build(stop=None):
    nc = bass.Bass("TRN2", target_bir_lowering=False,
                   detect_race_conditions=False)

    x_in = nc.dram_tensor("x", [T, D], F32, kind="ExternalInput").ap()
    w_proj = {}
    for nm in ("wq", "wk", "wv", "wg", "wo"):
        w_proj[nm] = nc.dram_tensor(nm, [2, NDC, 128, 1024], F32,
                                    kind="ExternalInput").ap()
    w1_in = nc.dram_tensor("w1", [128, NDC * 160], F32, kind="ExternalInput").ap()
    w2_in = nc.dram_tensor("w2", [32, 5 * D], F32, kind="ExternalInput").ap()
    dw1_in = nc.dram_tensor("dw1", [128, NDC * 64], F32, kind="ExternalInput").ap()
    dw2_in = nc.dram_tensor("dw2", [64, D], F32, kind="ExternalInput").ap()
    maa6_in = nc.dram_tensor("maa6", [128, 96], F32, kind="ExternalInput").ap()
    td_in = nc.dram_tensor("td", [128, 16], F32, kind="ExternalInput").ap()
    bg_in = nc.dram_tensor("bg", [128, 16], F32, kind="ExternalInput").ap()
    sg_in = nc.dram_tensor("sg", [128, 64], F32, kind="ExternalInput").ap()
    su_in = nc.dram_tensor("su", [128, 64], F32, kind="ExternalInput").ap()
    sd_in = nc.dram_tensor("sd", [1, D], F32, kind="ExternalInput").ap()
    wg8_in = nc.dram_tensor("wg8", [NIC, 128, NDC * 128], I8,
                            kind="ExternalInput").ap()
    wu8_in = nc.dram_tensor("wu8", [NIC, 128, NDC * 128], I8,
                            kind="ExternalInput").ap()
    wd8_in = nc.dram_tensor("wd8", [8192, D], I8, kind="ExternalInput").ap()
    id_in = nc.dram_tensor("idm", [128, 128], F32, kind="ExternalInput").ap()
    mask_in = nc.dram_tensor("maskm", [128, 128], F32, kind="ExternalInput").ap()
    out_hbm = nc.dram_tensor("out", [T, D], F32, kind="ExternalOutput").ap()
    wo_scr = nc.dram_tensor("wo_scr", [2, NDC, 128, 1024], BF16,
                            kind="Internal").ap()

    ctx = []

    def enter(g):
        ctx.append(g)
        return g.__enter__()

    arena = enter(nc.sbuf_tensor("arena", [128, ARENA], I8))
    ps = [enter(nc.psum_tensor(f"ps{i}", [128, 512], F32)) for i in range(8)]

    off = [0]

    def alloc(n, align=64):
        o = (off[0] + align - 1) // align * align
        off[0] = o + n
        return o

    def V(o, nbytes, dtype, parts=128, p0=0):
        return arena[p0:p0 + parts, o:o + nbytes].bitcast(dtype)

    o_x2 = alloc(32768)
    o_S = alloc(4096)
    o_Sbf = alloc(2048)
    o_id = alloc(256)
    o_mask = alloc(256)
    o_td = alloc(64)
    o_maa = alloc(384)
    o_bg = alloc(64)
    o_sg = alloc(256)
    o_su = alloc(256)
    o_sdb = alloc(8192)
    o_dw1 = alloc(2048)
    o_dw2 = alloc(4096)
    o_w1 = alloc(5120)
    o_w2b = alloc(4096)
    o_t6 = alloc(1024)
    o_t5 = alloc(5120)
    o_stat = alloc(128)
    o_enpc = alloc(128)
    o_eps = alloc(64)
    o_carry = alloc(64)
    o_wbuf = [alloc(4096), alloc(4096)]
    o_scrA = alloc(2048)
    o_scrB = alloc(2048)
    o_scrC = alloc(2048)
    o_dump = alloc(4096)
    o_ext = alloc(16384)
    o_d1 = alloc(16384)
    o_xf = alloc(16384)
    o_q = alloc(16384)
    o_k = alloc(16384)
    o_v = alloc(16384)
    o_g = alloc(16384)
    o_lt = alloc(8192)
    assert off[0] <= ARENA, f"arena overflow {off[0]}"

    o_gwb = [o_d1, o_d1 + 4096]
    o_uwb = [o_d1 + 8192, o_d1 + 12288]
    o_wdb = [o_lt, o_lt + 4096]
    o_mixt = [o_lt, o_lt + 1024]
    o_sgb = [o_scrA, o_scrB]

    x2 = V(o_x2, 32768, F32)
    Sv = V(o_S, 4096, F32)
    Sb = V(o_Sbf, 2048, BF16)
    idm = V(o_id, 256, BF16)
    maskm = V(o_mask, 256, BF16)
    tdv = V(o_td, 64, F32)
    maa = V(o_maa, 384, F32)
    bgv = V(o_bg, 64, F32)
    sgv = V(o_sg, 256, F32)
    suv = V(o_su, 256, F32)
    sdb = V(o_sdb, 8192, F32)
    dw1 = V(o_dw1, 2048, BF16)
    dw2 = V(o_dw2, 4096, BF16, parts=64)
    w1v = V(o_w1, 5120, BF16)
    w2b = V(o_w2b, 4096, BF16, parts=32)
    t6v = V(o_t6, 1024, BF16, parts=64)
    t5v = V(o_t5, 5120, BF16, parts=32)
    stat = V(o_stat, 128, F32)
    enpc = V(o_enpc, 64, F32)
    epsv = V(o_eps, 4, F32)
    carry = V(o_carry, 32, BF16)
    scrA = V(o_scrA, 2048, F32)
    scrB = V(o_scrB, 2048, F32)
    scrC = V(o_scrC, 2048, F32)
    dump = V(o_dump, 4096, BF16)
    dump2 = V(o_scrA, 4096, BF16)
    Pv = V(o_ext, 8192, F32)
    kdtm = V(o_ext + 8192, 4096, BF16)
    ffn_tm = V(o_ext, 16384, BF16)
    attn_tm = V(o_d1, 16384, BF16)
    vtm = V(o_v, 16384, BF16)

    def fm3(o, parts=128, p0=0):
        return V(o, 16384, BF16, parts, p0).rearrange("p (a b) -> p a b", a=16)

    ln1_3 = fm3(o_ext)
    d1_3 = fm3(o_d1)
    xf_3 = fm3(o_xf)
    q_3 = fm3(o_q)
    k_3 = fm3(o_k)
    g_3 = fm3(o_g)
    ln2_3 = fm3(o_g)

    def sl3(ap3, dc, a, b):
        return ap3[:, dc:dc + 1, a:b].squeeze(1)

    def fuse_ic(ic):
        base = [o_xf, o_q, o_k, o_v][ic // 16]
        return V(base + (ic % 16) * 1024, 1024, BF16)

    prog = Prog(nc)
    prog.make_sems(["T", "V", "A", "WC", "XC", "XI", "ST",
                    "W0", "W1", "W2", "W3", "W4", "W5", "W6", "W7", "W8",
                    "W9", "G0", "G1", "U0", "U1",
                    "D0", "D1", "D2", "D3", "D4", "D5", "P", "WB"])

    def mm(out, lhsT, rhs, start, stop, waits=(), inc=None, tp=None):
        def f(e, out=out, lhsT=lhsT, rhs=rhs, start=start, stop=stop, tp=tp):
            return e.matmul(out, lhsT, rhs, start=start, stop=stop,
                            tile_position=tp)
        return prog.emit("tensor", f, waits, inc)

    def tr(out, in_, waits=(), inc=None):
        def f(e, out=out, in_=in_):
            return e.transpose(out, in_, idm)
        return prog.emit("tensor", f, waits, inc)

    def ve(meth, *args, waits=(), inc=None, **kw):
        def f(e, meth=meth, args=args, kw=kw):
            return getattr(e, meth)(*args, **kw)
        return prog.emit("vector", f, waits, inc)

    def ac(meth, *args, waits=(), inc=None, **kw):
        def f(e, meth=meth, args=args, kw=kw):
            return getattr(e, meth)(*args, **kw)
        return prog.emit("scalar", f, waits, inc)

    def po(meth, *args, waits=(), inc=None, **kw):
        def f(e, meth=meth, args=args, kw=kw):
            return getattr(e, meth)(*args, **kw)
        return prog.emit("gpsimd", f, waits, inc)

    def dma_w(dst, src, waits=(), sem="WC"):
        def f(e, dst=dst, src=src):
            return e.dma_start(out=dst, in_=src)
        return prog.emit("gpsimd", f, waits, sem, 16)

    def dma_x(dst, src, waits=(), sem="XC"):
        def f(e, dst=dst, src=src):
            return e.dma_start(out=dst, in_=src)
        return prog.emit("sync", f, waits, sem, 16)

    def dma_wb(dst, src, waits=()):
        def f(e, dst=dst, src=src):
            return e.dma_start(out=dst, in_=src)
        return prog.emit("sync", f, waits, "WB", 16)

    def dma_st(dst, src, waits=()):
        def f(e, dst=dst, src=src):
            return e.dma_start(out=dst, in_=src)
        return prog.emit("sync", f, waits, "ST", 16)

    dbg_specs = []

    def stop_here(tag, pairs):
        if stop == tag:
            dbg_specs.extend(pairs)
            raise _StopEmit

    dbg_w = []
    if os.environ.get("KDBG0"):
        v0 = ve("memset", arena[:, 0:ARENA].bitcast(F32), 0.0, inc="V")
        dbg_w = [("V", v0)]
    dma_w(idm, id_in, waits=dbg_w)
    dma_w(maskm, mask_in)
    dma_w(dw1, dw1_in)
    dma_w(dw2, dw2_in)
    dma_w(w1v, w1_in)
    W_CONST = prog.val("WC")
    dma_x(tdv, td_in)
    dma_x(maa, maa6_in)
    dma_x(bgv, bg_in)
    dma_x(sgv, sg_in)
    dma_x(suv, su_in)
    sdrow = V(o_ext, 8192, F32, parts=1)
    ones1 = V(o_ext + 8192, 512, F32, parts=1)
    dma_x(sdrow, sd_in)
    X_CONST = prog.val("XC")
    ve("memset", ones1, 1.0)
    ve("memset", epsv, EPS)
    ve("memset", Sv, 0.0)
    ve("memset", Sb, 0.0)
    v_init = ve("memset", carry[:, 0:16], 0.0, inc="V")
    last_ev = 0
    for j in range(4):
        tmm = mm(ps[0][:, 0:512], ones1[0:1, 0:128],
                 sdrow[0:1, j * 512:(j + 1) * 512], True, True,
                 waits=[("XC", X_CONST), ("WC", W_CONST), ("V", v_init),
                        ("V", last_ev)],
                 inc="T")
        ve("tensor_copy", sdb[:, j * 512:(j + 1) * 512],
           ps[0][:, 0:512], waits=[("T", tmm)])
        last_ev = prog.emit("vector", lambda e: e.drain(), inc="V")
    SDB_V = last_ev

    n_stores = 0
    ms = {"ps67_free_V": SDB_V, "ps03_free_V": SDB_V, "sbf_V": 0,
          "ps67_free2_V": 0, "supd_free_V": 0, "t6_A": 0,
          "v_ssq": 0, "mix_stt": [0] * 8, "chainP": {},
          "chain_prevP": 0, "ps03_free": None}
    wbuf_last_T = [0, 0]
    gwb_last_T = [0, 0]
    uwb_last_T = [0, 0]
    ws_off = [o_wbuf[0] + 2048 * i for i in range(4)] + \
             [o_xf + 2048 * i for i in range(4)]
    WS_SEMS = ["W0", "W1", "W3", "W4", "W5", "W6", "W7", "W8"]
    ws_last_T = [0] * 8
    ws_wb_slot = [0] * 8
    wo_wb = {}
    w2_last_T = [0, 0]
    wd_off = [o_lt, o_lt + 4096, o_d1, o_d1 + 4096,
              o_d1 + 8192, o_d1 + 12288]
    WD_SEMS = ["D0", "D1", "D2", "D3", "D4", "D5"]
    wd_last_T = [0] * 6

    def ln_tc(src, tc, w0):
        a_sq = ac("activation", dump, src, AF.Square,
                  waits=w0 + [("V", ms["v_ssq"])], inc="A")
        ve("reduce_sum", stat[:, 4 + tc:5 + tc], src, AX.X, waits=w0)
        ve("reduce_sum", stat[:, tc:tc + 1], dump, AX.X,
           waits=[("A", a_sq)])
        v_ssq = prog.emit("vector", lambda e: e.drain(), inc="V")
        ms["v_ssq"] = v_ssq
        a_t1 = ac("activation", stat[:, 12 + tc:13 + tc],
                  stat[:, 4 + tc:5 + tc], AF.Square,
                  waits=[("V", v_ssq)], inc="A")
        ve("tensor_scalar", stat[:, 16 + tc:17 + tc],
           stat[:, 4 + tc:5 + tc], -1.0 / D, None, ALU.mult)
        v_var = ve("scalar_tensor_tensor", stat[:, 8 + tc:9 + tc],
                   stat[:, tc:tc + 1], float(D), stat[:, 12 + tc:13 + tc],
                   ALU.mult, ALU.subtract, waits=[("A", a_t1)], inc="V")
        a_std = ac("activation", stat[:, 8 + tc:9 + tc],
                   stat[:, 8 + tc:9 + tc], AF.Sqrt, bias=epsv,
                   scale=1.0 / (D * D), waits=[("V", v_var)], inc="A")
        ve("reciprocal", stat[:, 8 + tc:9 + tc], stat[:, 8 + tc:9 + tc],
           waits=[("A", a_std)])
        v_rc = prog.emit("vector", lambda e: e.drain(), inc="V")
        a_nm = ac("activation", stat[:, 12 + tc:13 + tc],
                  stat[:, 8 + tc:9 + tc], AF.Copy,
                  scale=stat[:, 16 + tc:17 + tc], waits=[("V", v_rc)],
                  inc="A")
        return a_nm

    def ln_pair(jobs):
        dbs = [dump, dump2]
        a_sq = []
        for (src, tc, w0), db in zip(jobs, dbs):
            a_sq.append(ac("activation", db, src, AF.Square,
                           waits=list(w0) + [("V", ms["v_ssq"])], inc="A"))
        for (src, tc, w0), db in zip(jobs, dbs):
            ve("reduce_sum", stat[:, 4 + tc:5 + tc], src, AX.X,
               waits=list(w0))
        for i, ((src, tc, w0), db) in enumerate(zip(jobs, dbs)):
            ve("reduce_sum", stat[:, tc:tc + 1], db, AX.X,
               waits=[("A", a_sq[i])])
        v_ssq = prog.emit("vector", lambda e: e.drain(), inc="V")
        ms["v_ssq"] = v_ssq
        a_t1 = []
        for (src, tc, w0) in jobs:
            a_t1.append(ac("activation", stat[:, 12 + tc:13 + tc],
                           stat[:, 4 + tc:5 + tc], AF.Square,
                           waits=[("V", v_ssq)], inc="A"))
        for (src, tc, w0) in jobs:
            ve("tensor_scalar", stat[:, 16 + tc:17 + tc],
               stat[:, 4 + tc:5 + tc], -1.0 / D, None, ALU.mult)
        v_var = []
        for i, (src, tc, w0) in enumerate(jobs):
            v_var.append(ve("scalar_tensor_tensor", stat[:, 8 + tc:9 + tc],
                            stat[:, tc:tc + 1], float(D),
                            stat[:, 12 + tc:13 + tc],
                            ALU.mult, ALU.subtract,
                            waits=[("A", a_t1[i])], inc="V"))
        a_std = []
        for i, (src, tc, w0) in enumerate(jobs):
            a_std.append(ac("activation", stat[:, 8 + tc:9 + tc],
                            stat[:, 8 + tc:9 + tc], AF.Sqrt, bias=epsv,
                            scale=1.0 / (D * D), waits=[("V", v_var[i])],
                            inc="A"))
        for i, (src, tc, w0) in enumerate(jobs):
            ve("reciprocal", stat[:, 8 + tc:9 + tc],
               stat[:, 8 + tc:9 + tc], waits=[("A", a_std[i])])
        v_rc = prog.emit("vector", lambda e: e.drain(), inc="V")
        a_nm = []
        for (src, tc, w0) in jobs:
            a_nm.append(ac("activation", stat[:, 12 + tc:13 + tc],
                           stat[:, 8 + tc:9 + tc], AF.Copy,
                           scale=stat[:, 16 + tc:17 + tc],
                           waits=[("V", v_rc)], inc="A"))
        return a_nm

    ln1_state = {}

    def emit_ln1(it2, tc, extra_w=()):
        r0b = it2 * PT
        xv = dma_x(x2[:, tc * D:(tc + 1) * D],
                   x_in[r0b + tc * 128:r0b + (tc + 1) * 128, :],
                   sem="XI")
        a_nm = ln_tc(x2[:, tc * D:(tc + 1) * D], tc, [("XI", xv)])
        lt = V(o_g + tc * 4096, 4096, BF16)
        ln1_state[(it2, tc)] = ve("tensor_scalar", lt,
                                  x2[:, tc * D:(tc + 1) * D],
                                  stat[:, 8 + tc:9 + tc],
                                  stat[:, 12 + tc:13 + tc], ALU.mult,
                                  ALU.add,
                                  waits=[("A", a_nm)] + list(extra_w),
                                  inc="V")

    try:
      for it in range(NTILE):
        r0 = it * PT
        xs = [x2[:, tc * D:(tc + 1) * D] for tc in range(4)]
        if it == 0:
            for pr in range(2):
                tcs = [2 * pr, 2 * pr + 1]
                xvs = {t: dma_x(x2[:, t * D:(t + 1) * D],
                                x_in[t * 128:(t + 1) * 128, :], sem="XI")
                       for t in tcs}
                a_l = ln_pair([(x2[:, t * D:(t + 1) * D], t,
                                [("XI", xvs[t])]) for t in tcs])
                for i, t in enumerate(tcs):
                    lt = V(o_g + t * 4096, 4096, BF16)
                    ln1_state[(0, t)] = ve(
                        "tensor_scalar", lt, x2[:, t * D:(t + 1) * D],
                        stat[:, 8 + t:9 + t], stat[:, 12 + t:13 + t],
                        ALU.mult, ALU.add, waits=[("A", a_l[i])], inc="V")

        ev_bank = [ms["ps67_free_V"], ms["ps67_free_V"]]
        for tc in range(4):
            v_n = ln1_state[(it, tc)]
            lt = V(o_g + tc * 4096, 4096, BF16)
            for dc in range(NDC):
                bank = 6 + dc // 8
                j = dc % 8
                pbf = ps[bank][:, 0:512].bitcast(BF16)
                tr(pbf[:, j * 128:(j + 1) * 128],
                   lt[:, dc * 128:(dc + 1) * 128],
                   waits=([("V", v_n), ("V", ev_bank[dc // 8])]
                          if j == 0 else []),
                   inc="T" if j == 7 else None)
            tw = prog.val("T")
            for half in range(2):
                pbf = ps[6 + half][:, 0:512].bitcast(BF16)
                for j in range(8):
                    dc = half * 8 + j
                    w0 = []
                    if half == 0 and j == 0:
                        w0 = [("T", tw), ("A", ms.get("ffnA", 0))]
                    ve("tensor_copy", sl3(ln1_3, dc, tc * 128, (tc + 1) * 128),
                       pbf[:, j * 128:(j + 1) * 128], waits=w0,
                       inc="V" if j == 7 else None)
                ev_bank[half] = prog.val("V")
        ms["ps67_free_V"] = prog.val("V")

        ve("tensor_tensor", d1_3[:, :, 0:1], carry[:, 0:16].unsqueeze(2),
           ln1_3[:, :, 0:1], ALU.subtract)
        v_d1 = ve("tensor_tensor", d1_3[:, :, 1:512], ln1_3[:, :, 0:511],
                  ln1_3[:, :, 1:512], ALU.subtract, inc="V")
        xxx_v = [0] * NDC
        for dc in range(NDC):
            xxx_v[dc] = ve("scalar_tensor_tensor", sl3(xf_3, dc, 0, 512),
                           sl3(d1_3, dc, 0, 512), maa[:, dc:dc + 1],
                           sl3(ln1_3, dc, 0, 512), ALU.mult, ALU.add,
                           waits=([("V", v_d1), ("XC", X_CONST),
                                   ("WC", W_CONST)]
                                  if dc == 0 else []), inc="V")
        v_xxx = ve("tensor_copy", carry[:, 0:16],
                   ln1_3[:, :, 511:512].squeeze(2), inc="V")

        t_t5 = 0
        for f in range(5):
            for dc in range(NDC):
                w0t = [("V", xxx_v[dc])] if f == 0 else []
                if f == 0 and dc == 0:
                    w0t += [("WC", W_CONST), ("V", ms["ps03_free_V"]),
                            ("A", prog.val("A"))]
                t_t5 = mm(ps[f][0:32, 0:512],
                          w1v[:, dc * 160 + f * 32: dc * 160 + (f + 1) * 32],
                          sl3(xf_3, dc, 0, 512), dc == 0, dc == 15,
                          waits=w0t, inc="T" if dc == 15 else None)
        a_t5 = 0
        for f in range(5):
            a_t5 = ac("activation", t5v[:, f * 512:(f + 1) * 512],
                      ps[f][0:32, 0:512], AF.Tanh,
                      waits=[("T", t_t5)], inc="A")

        qe = fm3(o_q, parts=64, p0=0)
        qo = fm3(o_q, parts=64, p0=64)
        ke = fm3(o_k, parts=64, p0=0)
        ko = fm3(o_k, parts=64, p0=64)

        def kdtm_v(tc):
            return V(o_ext + 8192 + (tc % 2) * 4096, 4096, BF16)

        def enpc_v(tc):
            return V(o_enpc + (tc % 2) * 64, 64, F32)

        def emit_decay2_round(tc, rnd, extra):
            t_vals = []
            for gg in range(2):
                g = 2 * rnd + gg
                t = 0
                for j in range(4):
                    oc = g * 4 + j
                    w0 = list(extra) if (gg == 0 and j == 0) else []
                    t = mm(ps[gg][:, j * 128:(j + 1) * 128],
                           dw2[:, oc * 128:(oc + 1) * 128],
                           t6v[:, tc * 128:(tc + 1) * 128], True, True,
                           waits=w0, inc="T" if j == 3 else None)
                t_vals.append(t)
            return t_vals

        def emit_chain_group(tc, g, t_dec_g):
            c0 = tc * 128
            bank = g % 2
            psg3 = ps[bank][:, 0:512].rearrange("p (a b) -> p a b", a=4)
            tdb = tdv[:, g * 4:(g + 1) * 4].unsqueeze(2) \
                .broadcast_to([128, 4, 128])
            scrB3 = scrB.rearrange("p (a b) -> p a b", a=4)
            v1 = ve("tensor_tensor", scrB3, psg3, tdb, ALU.add,
                    waits=[("T", t_dec_g), ("A", prog.val("A")),
                           ("P", ms["chain_prevP"])], inc="V")
            a1 = ac("activation", scrB, scrB, AF.Exp, waits=[("V", v1)],
                    inc="A")
            for j in range(4):
                dc = g * 4 + j
                ve("tensor_tensor_scan", Pv[:, dc * 128:(dc + 1) * 128],
                   scrB[:, j * 128:(j + 1) * 128],
                   scrB[:, j * 128:(j + 1) * 128], 0.0, ALU.add, ALU.bypass,
                   waits=[("A", a1)] if j == 0 else [])
            v_sc = prog.emit("vector", lambda e: e.drain(), inc="V")
            a_em = ac("activation", scrA, scrB, AF.Exp, scale=-1.0,
                      waits=[("A", a1), ("P", ms["chain_prevP"])], inc="A")
            a_eP = ac("activation", scrC, Pv[:, g * 512:(g + 1) * 512],
                      AF.Exp, waits=[("V", v_sc)], inc="A")
            ve("tensor_tensor", scrA, scrA, scrC, ALU.mult,
               waits=[("A", a_eP)])
            ve("tensor_tensor", scrC, scrC, scrA, ALU.subtract)
            v_kds = prog.emit("vector", lambda e: e.drain(), inc="V")
            scrC3 = scrC.rearrange("p (a b) -> p a b", a=4)
            po("tensor_tensor", k_3[:, g * 4:(g + 1) * 4, c0:c0 + 128],
               k_3[:, g * 4:(g + 1) * 4, c0:c0 + 128], scrC3, ALU.mult,
               waits=[("V", v_kds)])
            a_eN = ac("activation", scrA, Pv[:, g * 512:(g + 1) * 512],
                      AF.Exp, scale=-1.0, waits=[("V", v_kds)], inc="A")
            scrA3 = scrA.rearrange("p (a b) -> p a b", a=4)
            po("tensor_tensor", q_3[:, g * 4:(g + 1) * 4, c0:c0 + 128],
               q_3[:, g * 4:(g + 1) * 4, c0:c0 + 128], scrA3,
               ALU.mult, waits=[("A", a_eN)])
            v_q = prog.emit("gpsimd", lambda e: e.drain(), inc="P")
            ms["chainP"][(tc, g)] = v_q
            ms["chain_prevP"] = v_q
            return v1, v_q

        def emit_kdT(tc, v_scales):
            c0 = tc * 128
            t_kt = 0
            for dc in range(NDC):
                bank = dc // 8
                j = dc % 8
                pbf = ps[bank][:, 0:512].bitcast(BF16)
                t_kt = tr(pbf[:, j * 128:(j + 1) * 128],
                          sl3(k_3, dc, c0, c0 + 128),
                          waits=[("P", v_scales)] if dc == 0 else [],
                          inc="T" if j == 7 else None)
            return t_kt

        def emit_kdT_A(tc, t_kt):
            kv = kdtm_v(tc)
            vv = 0
            for half in range(2):
                pbf = ps[half][:, 0:512].bitcast(BF16)
                for j in range(8):
                    dc = half * 8 + j
                    vv = ac("activation", kv[:, dc * 128:(dc + 1) * 128],
                            pbf[:, j * 128:(j + 1) * 128], AF.Copy,
                            waits=[("T", t_kt)]
                            if (half == 0 and j == 0) else [],
                            inc="A" if (half == 1 and j == 7) else None)
            return vv

        att_pro = []

        def exec_prologue():
            td0 = emit_decay2_round(0, 0, [("A", ms["t6_A"]),
                                           ("V", ms["mix_stt"][0]),
                                           ("V", ms["mix_stt"][1])])
            v1_a, _ = emit_chain_group(0, 0, td0[0])
            v1_b, _ = emit_chain_group(0, 1, td0[1])
            td1 = emit_decay2_round(0, 1, [("V", v1_a), ("V", v1_b)])
            emit_chain_group(0, 2, td1[0])
            _, vq0 = emit_chain_group(0, 3, td1[1])
            a_pc = {0: ac("activation", enpc_v(0)[:, 0:16], Pv[:, 127::128],
                              AF.Exp, scale=-1.0, waits=[("P", vq0)], inc="A")}
            v_ktm = {0: emit_kdT_A(0, emit_kdT(0, vq0))}


            ms["pro"] = (a_pc, v_ktm)

        PROJ = [("r", "wq"), ("k", "wk"), ("v", "wv"), ("w", None),
                ("g", "wg")]
        def w2chunk(fi2, rnd):
            dst = V(o_w2b + rnd * 2048, 2048, BF16, parts=32)
            return dma_w(dst, w2_in[:, fi2 * D + rnd * 1024:
                                    fi2 * D + (rnd + 1) * 1024],
                         waits=[("T", w2_last_T[rnd])],
                         sem=("W2" if rnd == 0 else "W9"))

        w2v = [w2chunk(0, 0), w2chunk(0, 1)]
        for fi, (fname, wname) in enumerate(PROJ):
            fidx = fi + 1
            mix_V = [0] * NDC
            v_round = prog.val("V")
            mixt8 = V(o_lt, 8192, BF16)
            mixt83 = mixt8.rearrange("p (a b) -> p a b", a=8)
            for rnd in range(2):
                t_m = 0
                for j in range(8):
                    dc = rnd * 8 + j
                    if j == 0:
                        w0 = [("W2" if rnd == 0 else "W9", w2v[rnd]),
                              ("A", prog.val("A"))]
                        if rnd == 0:
                            w0.append(("V", v_round))
                    else:
                        w0 = []
                    if rnd == 1:
                        w0.append(("V", ms["mix_stt"][j]))
                    t_m = mm(ps[j][:, 0:512],
                             w2b[:, dc * 128:(dc + 1) * 128],
                             t5v[:, fi * 512:(fi + 1) * 512], True, True,
                             waits=w0, inc="T" if j == 7 else None)
                w2_last_T[rnd] = t_m
                for j in range(8):
                    dc = rnd * 8 + j
                    ms["mix_stt"][j] = ve(
                        "scalar_tensor_tensor",
                        mixt8[:, j * 512:(j + 1) * 512],
                        ps[j][:, 0:512],
                        maa[:, fidx * 16 + dc:fidx * 16 + dc + 1],
                        sl3(d1_3, dc, 0, 512), ALU.add, ALU.mult,
                        waits=[("T", t_m)] if j == 0 else [], inc="V")
                for hf in range(2):
                    j0 = rnd * 8 + hf * 4
                    v_round2 = ve("tensor_tensor",
                                  xf_3[:, j0:j0 + 4, :],
                                  mixt83[:, hf * 4:hf * 4 + 4],
                                  ln1_3[:, j0:j0 + 4, :], ALU.add,
                                  inc="V")
                    for j in range(4):
                        mix_V[j0 + j] = v_round2
            if fi + 1 < len(PROJ):
                w2v = [w2chunk(fi + 1, 0), w2chunk(fi + 1, 1)]
            mixes_done = prog.val("V")

            if fname == "w":
                t_t6 = 0
                for dc in range(NDC):
                    t_t6 = mm(ps[7][0:64, 0:512], dw1[:, dc * 64:(dc + 1) * 64],
                              sl3(xf_3, dc, 0, 512), dc == 0, dc == 15,
                              waits=([("V", mixes_done),
                                      ("A", prog.val("A"))]
                                     if dc == 0 else []),
                              inc="T" if dc == 15 else None)
                ms["t6_A"] = ac("activation", t6v, ps[7][0:64, 0:512], AF.Tanh,
                                waits=[("T", t_t6)], inc="A")
                continue

            if fname == "v":
                ev = 0
                for half in range(2):
                    for dc in range(NDC):
                        sl = (half * 16 + dc) % 4
                        wbv = V(ws_off[sl], 2048, BF16)
                        wW2 = dma_w(wbv, w_proj["wv"][half, dc],
                                    waits=[("T", ws_last_T[sl])],
                                    sem=WS_SEMS[sl])
                        for tc in range(4):
                            for og in range(2):
                                w0 = []
                                if tc == 0 and og == 0:
                                    w0 = [(WS_SEMS[sl], wW2),
                                          ("V", mix_V[dc]), ("A", ev)]
                                    if half == 0 and dc == 0:
                                        w0 += [("V", mixes_done),
                                               ("A", prog.val("A"))]
                                mm(ps[tc * 2 + og][:, 0:512],
                                   sl3(xf_3, dc, tc * 128,
                                       (tc + 1) * 128),
                                   wbv[:, og * 512:(og + 1) * 512],
                                   dc == 0, dc == 15, waits=w0,
                                   inc=("T" if (tc == 3 and og == 1)
                                        else None))
                        ws_last_T[sl] = prog.val("T")
                    tw = prog.val("T")
                    for tc in range(4):
                        for og in range(2):
                            o_col = tc * D + (half * 2 + og) * 512
                            ev2 = ac("activation",
                                     vtm[:, o_col:o_col + 512],
                                     ps[tc * 2 + og][:, 0:512], AF.Copy,
                                     waits=[("T", tw)]
                                     if (tc == 0 and og == 0) else [],
                                     inc=("A" if (tc == 3 and og == 1)
                                          else None))
                            if tc == 3 and og == 1:
                                ev = ev2
                continue

            if fname == "g":
                exec_prologue()
            proj_gate_V = prog.val("V")
            ev = 0
            eva = 0
            for half in range(2):
                for dc in range(NDC):
                    sl = (half * 16 + dc) % 4
                    wbv = V(ws_off[sl], 2048, BF16)
                    wW2 = dma_w(wbv, w_proj[wname][half, dc],
                                waits=[("T", ws_last_T[sl])],
                                sem=WS_SEMS[sl])
                    for oc in range(8):
                        w0 = []
                        if oc == 0:
                            w0 = [(WS_SEMS[sl], wW2), ("V", mix_V[dc]),
                                  ("A", ev), ("A", eva)]
                            if half == 0 and dc == 0:
                                w0 += [("V", proj_gate_V),
                                       ("A", prog.val("A"))]
                        mm(ps[oc][:, 0:512],
                           wbv[:, oc * 128:(oc + 1) * 128],
                           sl3(xf_3, dc, 0, 512),
                           dc == 0, dc == 15, waits=w0,
                           inc=("T" if oc == 7 else None))
                    ws_last_T[sl] = prog.val("T")
                tw = prog.val("T")
                for oc in range(8):
                    ocg = half * 8 + oc
                    if fname == "g":
                        eva = ac("activation", sl3(g_3, ocg, 0, 512),
                                 ps[oc][:, 0:512], AF.Silu,
                                 bias=bgv[:, ocg:ocg + 1],
                                 waits=[("T", tw)] if oc == 0 else [],
                                 inc="A" if oc == 7 else None)
                    else:
                        dst3 = q_3 if fname == "r" else k_3
                        ev2 = ac("activation", sl3(dst3, ocg, 0, 512),
                                 ps[oc][:, 0:512], AF.Copy,
                                 waits=[("T", tw)] if oc == 0 else [],
                                 inc="A" if oc == 7 else None)
                        if oc == 7:
                            ev = ev2

        PROJ_DONE_V = prog.val("V")
        PROJ_DONE_A = prog.val("A")
        PROJ_DONE_T = prog.val("T")
        for _i in range(4, 8):
            ws_last_T[_i] = PROJ_DONE_T
        if it == 0:
            stop_here("pg1", [
                ("stat", stat), ("x2d", x2),
                ("ln1", V(o_ext, 16384, BF16)), ("d1", V(o_d1, 16384, BF16)),
                ("xg", V(o_xf, 16384, BF16)), ("q", V(o_q, 16384, BF16)),
                ("k", V(o_k, 16384, BF16)), ("v", vtm),
                ("g", V(o_g, 16384, BF16)), ("t6", t6v), ("t5", t5v)])

        a_pc, v_ktm = ms["pro"]

        atm_free = [prog.val("V")] * 2
        yg_free = [prog.val("A")] * 2
        supd_T = [0] * 4
        supd_v = list(ms.get("supd_groups", [0, 0, 0, 0]))
        sbf_prev = ms["sbf_V"]

        def emit_AT(tc, p):
            c0 = tc * 128
            t_at = 0
            psb = ps[4 + p % 2][:, 0:512]
            for idx, dc in ((0, 2 * p), (1, 2 * p + 1)):
                off = idx * 256
                w0 = []
                if idx == 0:
                    w0 = [("V", atm_free[p % 2]),
                          ("P", ms["chainP"][(tc, p // 2)])]
                    if p == 0 and tc == 0:
                        w0 += [("A", v_ktm[0]), ("A", PROJ_DONE_A)]
                mm(psb[:, off:off + 128],
                   sl3(ke, dc, c0, c0 + 128), sl3(qe, dc, c0, c0 + 128),
                   True, True, waits=w0, tp=(0, 0))
                mm(psb[0:64, off + 128:off + 256],
                   sl3(ko, dc, c0, c0 + 128)[:, 0:64],
                   sl3(qo, dc, c0, c0 + 128), True, True, tp=(64, 0))
                t_at = mm(psb[64:128, off + 128:off + 256],
                          sl3(ko, dc, c0, c0 + 128)[:, 64:128],
                          sl3(qo, dc, c0, c0 + 128), True, True,
                          tp=(64, 64), inc="T" if idx == 1 else None)
            return t_at

        pairs = [(tc, p) for tc in range(4) for p in range(8)]
        at_T = {pairs[0]: emit_AT(*pairs[0])}
        nxt = {}
        for i, (tc, p) in enumerate(pairs):
            c0 = tc * 128
            kv = kdtm_v(tc)
            dc0, dc1 = 2 * p, 2 * p + 1
            if i + 1 < len(pairs):
                at_T[pairs[i + 1]] = emit_AT(*pairs[i + 1])
            psb = ps[4 + p % 2][:, 0:512]
            t_at = at_T[(tc, p)]
            if tc < 3 and p == 0:
                nxt = {}
                nxt["td0"] = emit_decay2_round(tc + 1, 0,
                                               [("A", v_ktm[tc])])
            if tc < 3 and p == 2:
                nxt["td1"] = emit_decay2_round(
                    tc + 1, 1, [("V", nxt["v1_0"]), ("V", nxt["v1_1"])])
            if tc < 3 and p == 5:
                nxt["t_kt"] = emit_kdT(tc + 1, nxt["vq"])
            for idx, dc in ((0, dc0), (1, dc1)):
                he = 2 * dc
                g = dc // 4
                bS = 2 + g // 2
                base = (g % 2) * 256 + (dc % 4) * 64
                w0s = []
                if dc % 4 == 0:
                    w0s = [("V", supd_v[g]), ("A", v_ktm[tc])]
                mm(ps[bS][0:64, base:base + 64],
                   kv[:, dc * 128:dc * 128 + 64],
                   vtm[:, tc * D + he * 64: tc * D + (he + 1) * 64],
                   True, True, waits=w0s, tp=(0, 0))
                supd_T[g] = mm(ps[bS][64:128, base:base + 64],
                               kv[:, dc * 128 + 64:dc * 128 + 128],
                               vtm[:, tc * D + (he + 1) * 64:
                                   tc * D + (he + 2) * 64],
                               True, True, tp=(0, 64),
                               inc="T" if dc % 4 == 3 else None)
            atmv = V(o_dump + (p % 2) * 1024, 1024, BF16)
            atm3 = atmv.rearrange("p (a b) -> p a b", a=4)
            ps3a = psb.rearrange("p (a b) -> p a b", a=4)
            mk3 = maskm.unsqueeze(1).broadcast_to([128, 2, 128])
            hs = []
            for idx in range(2):
                hs.append(ve("tensor_tensor", atm3[:, 2 * idx:2 * idx + 2],
                             ps3a[:, 2 * idx:2 * idx + 2], mk3, ALU.mult,
                             waits=[("T", t_at)] if idx == 0 else [],
                             inc="V"))
            atm_free[p % 2] = hs[1]
            bY = 6 + p % 2
            t_y = 0
            for idx, dc in ((0, dc0), (1, dc1)):
                off = idx * 128
                he = 2 * dc
                for rw in range(2):
                    r0_, r1_ = 64 * rw, 64 * (rw + 1)
                    w0 = []
                    if rw == 0 and idx == 0:
                        w0.append(("A", yg_free[p % 2]))
                        if p == 0:
                            w0.append(("V", sbf_prev))
                    mm(ps[bY][r0_:r1_, off:off + 128],
                       Sb[r0_:r1_, dc * 64:(dc + 1) * 64],
                       sl3(qe if rw == 0 else qo, dc, c0, c0 + 128),
                       True, False, waits=w0,
                       tp=(0, 0) if rw == 0 else (64, 64))
                    t_y = mm(ps[bY][r0_:r1_, off:off + 128],
                             vtm[:, tc * D + (he + rw) * 64:
                                 tc * D + (he + rw + 1) * 64],
                             atmv[:, idx * 256 + rw * 128:
                                  idx * 256 + rw * 128 + 128],
                             False, True, waits=[("V", hs[idx])],
                             tp=(0, 0) if rw == 0 else (0, 64),
                             inc="T" if (rw == 1 and idx == 1) else None)
            ystage = V(o_dump + 2048 + (p % 2) * 512, 512, BF16)
            yst3 = ystage.rearrange("p (a b) -> p a b", a=2)
            g3sl = g_3[:, dc0:dc0 + 2, c0:c0 + 128]
            yg_free[p % 2] = ac("activation", ystage,
                                ps[bY][:, 0:256].rearrange(
                                    "p (a b) -> p a b", a=2),
                                AF.Copy, waits=[("T", t_y)], inc="A")
            po("tensor_tensor", g3sl, yst3, g3sl, ALU.mult,
               waits=[("A", yg_free[p % 2])])
            if p == 7:
                ms["ygP"] = prog.emit("gpsimd", lambda e: e.drain(),
                                      inc="P")
            if tc < 3:
                if p == 0:
                    nxt["v1_0"], _ = emit_chain_group(tc + 1, 0,
                                                      nxt["td0"][0])
                elif p == 1:
                    nxt["v1_1"], _ = emit_chain_group(tc + 1, 1,
                                                      nxt["td0"][1])
                elif p == 2:
                    emit_chain_group(tc + 1, 2, nxt["td1"][0])
                elif p == 3:
                    _, nxt["vq"] = emit_chain_group(tc + 1, 3,
                                                    nxt["td1"][1])
                    a_pc[tc + 1] = ac("activation",
                                      enpc_v(tc + 1)[:, 0:16],
                                      Pv[:, 127::128], AF.Exp,
                                      scale=-1.0,
                                      waits=[("P", nxt["vq"])], inc="A")
            if p % 2 == 1:
                g = p // 2
                bS = 2 + g // 2
                gb = (g % 2) * 256
                s_sl = Sv[:, g * 256:(g + 1) * 256]
                ve("tensor_tensor", s_sl, s_sl,
                   ps[bS][:, gb:gb + 256], ALU.add,
                   waits=[("T", supd_T[g]), ("T", t_y),
                          ("A", a_pc[tc])])
                s3 = s_sl.rearrange("p (a b) -> p a b", a=4)
                eb = enpc_v(tc)[:, g * 4:(g + 1) * 4] \
                    .unsqueeze(2).broadcast_to([128, 4, 64])
                sb3 = Sb[:, g * 256:(g + 1) * 256] \
                    .rearrange("p (a b) -> p a b", a=4)
                ve("tensor_tensor", sb3, s3, eb, ALU.mult)
                ve("tensor_tensor", s3, s3, eb, ALU.mult)
                supd_v[g] = prog.emit("vector", lambda e: e.drain(),
                                      inc="V")
            if p == 6 and tc < 3:
                v_ktm[tc + 1] = emit_kdT_A(tc + 1, nxt["t_kt"])
            if p == 7:
                sbf_prev = prog.val("V")
                if tc == 1:
                    ms["ygP_tc1"] = ms["ygP"]
        ms["sbf_V"] = prog.val("V")
        ms["supd_groups"] = list(supd_v)
        ms["ps67_free2_V"] = prog.val("V")
        YG_V = prog.val("V")
        ms["ps03_free_V"] = YG_V
        if it == 0:
            stop_here("pg2", [("yg", V(o_g, 16384, BF16))])

        ev = 0
        v_res = 0
        ev_bank = [ms["ps67_free2_V"], ms["ps67_free2_V"]]
        ln2_done = [0, 0, 0, 0]
        wo_rel = [("A", v_ktm[3]), ("A", v_ktm[3]),
                  ("V", supd_v[1]), ("V", supd_v[3]),
                  ("V", atm_free[0]), ("V", atm_free[1]),
                  ("A", yg_free[0]), ("A", yg_free[1])]
        for pas in range(2):
            for dc in range(NDC):
                for half in range(2):
                    sl = (pas * 32 + dc * 2 + half) % 8
                    wbv = V(ws_off[sl], 2048, BF16)
                    w0d = [("T", ws_last_T[sl]),
                           ("WB", ws_wb_slot[sl])]
                    if it == 0 and pas == 0:
                        wWx = dma_w(wbv, w_proj["wo"][half, dc],
                                    waits=w0d, sem=WS_SEMS[sl])
                    else:
                        w0d.append(("WB", wo_wb[(half, dc)]))
                        wWx = dma_w(wbv, wo_scr[half, dc],
                                    waits=w0d, sem=WS_SEMS[sl])
                    for tcl in range(2):
                        tc = 2 * pas + tcl
                        for og in range(2):
                            bank = tcl * 4 + half * 2 + og
                            w0 = []
                            if tcl == 0 and og == 0:
                                w0 = [(WS_SEMS[sl], wWx)]
                            if dc == 0:
                                w0.append(wo_rel[bank] if pas == 0
                                          else ("V", ev))
                                if half == 0 and tcl == 0 and og == 0:
                                    w0.append(("P", ms["ygP"] if pas == 1
                                               else ms["ygP_tc1"]))
                                if pas == 0 and half == 0 and tcl == 0 \
                                        and og == 0:
                                    w0.append(("A", prog.val("A")))
                            mm(ps[bank][:, 0:512],
                               sl3(g_3, dc, tc * 128, (tc + 1) * 128),
                               wbv[:, og * 512:(og + 1) * 512],
                               dc == 0, dc == 15, waits=w0,
                               inc=("T" if (tcl == 1 and og == 1)
                                    else None))
                    ws_last_T[sl] = prog.val("T")
                    if it == 0 and pas == 0:
                        wo_wb[(half, dc)] = dma_wb(
                            wo_scr[half, dc], wbv,
                            waits=[("T", ws_last_T[sl])])
                        ws_wb_slot[sl] = wo_wb[(half, dc)]
            tw = prog.val("T")
            for tcl in range(2):
                tc = 2 * pas + tcl
                for half in range(2):
                    for og in range(2):
                        bank = tcl * 4 + half * 2 + og
                        o_col = tc * D + (half * 2 + og) * 512
                        ev = ve("tensor_copy", attn_tm[:, o_col:o_col + 512],
                                ps[bank][:, 0:512],
                                waits=[("T", tw)]
                                if (tcl == 0 and half == 0 and og == 0)
                                else [],
                                inc=("V" if (tcl == 1 and half == 1
                                             and og == 1) else None))
            if pas == 1:
                ms["ps67_free2_V"] = ev
            chain_tc = [2 * pas, 2 * pas + 1]
            a_at = ln_pair([(attn_tm[:, t * D:(t + 1) * D], t,
                             [("V", ev)] + ([("P", ms["ygP"])] if i2 == 0
                                            else []))
                            for i2, t in enumerate(chain_tc)])
            for i, t in enumerate(chain_tc):
                at_s = attn_tm[:, t * D:(t + 1) * D]
                ve("tensor_scalar", at_s, at_s, stat[:, 8 + t:9 + t],
                   stat[:, 12 + t:13 + t], ALU.mult, ALU.add,
                   waits=[("A", a_at[i])])
                ve("tensor_tensor", xs[t], xs[t], at_s, ALU.add)
            v_res = prog.emit("vector", lambda e: e.drain(), inc="V")
            a_l2 = ln_pair([(xs[t], t, [("V", v_res)]) for t in chain_tc])
            for i, t in enumerate(chain_tc):
                lt = V(o_d1 + t * 4096, 4096, BF16)
                ln2_done[t] = ve("tensor_scalar", lt, xs[t],
                                 stat[:, 8 + t:9 + t],
                                 stat[:, 12 + t:13 + t],
                                 ALU.mult, ALU.add,
                                 waits=[("A", a_l2[i])], inc="V")
        if it == 0:
            stop_here("lnattn", [("x2", x2)])
        for tc in range(4):
            lt = V(o_d1 + tc * 4096, 4096, BF16)
            for dc in range(NDC):
                bank = 6 + dc // 8
                j = dc % 8
                pbf = ps[bank][:, 0:512].bitcast(BF16)
                tr(pbf[:, j * 128:(j + 1) * 128],
                   lt[:, dc * 128:(dc + 1) * 128],
                   waits=([("V", ln2_done[tc]), ("V", ev_bank[dc // 8]),
                           ("V", ev)] if j == 0 else []),
                   inc="T" if j == 7 else None)
            tw = prog.val("T")
            for half in range(2):
                pbf = ps[6 + half][:, 0:512].bitcast(BF16)
                for j in range(8):
                    dc = half * 8 + j
                    ve("tensor_copy", sl3(ln2_3, dc, tc * 128, (tc + 1) * 128),
                       pbf[:, j * 128:(j + 1) * 128],
                       waits=[("T", tw)] if (half == 0 and j == 0) else [],
                       inc="V" if j == 7 else None)
                ev_bank[half] = prog.val("V")
        LN2T_T = prog.val("T")
        LN2_V = prog.val("V")
        if it == 0:
            stop_here("ln2", [("ln2", V(o_g, 16384, BF16))])

        ev_pair = [prog.val("A"), prog.val("A")]
        for ic in range(NIC):
            r = ic % 2
            gwv = V(o_gwb[r], 4096, BF16)
            uwv = V(o_uwb[r], 4096, BF16)
            dwait = [("T", gwb_last_T[r])]
            if ic < 2:
                dwait.append(("T", LN2T_T))
            wWg = dma_w(gwv, wg8_in[ic], waits=dwait, sem=f"G{r}")
            dwait = [("T", uwb_last_T[r])]
            if ic < 2:
                dwait.append(("T", LN2T_T))
            wWu = dma_w(uwv, wu8_in[ic], waits=dwait, sem=f"U{r}")
            bG, bU = 2 * r, 2 * r + 1
            t_g = 0
            for dc in range(NDC):
                t_g = mm(ps[bG][:, 0:512], gwv[:, dc * 128:(dc + 1) * 128],
                         sl3(ln2_3, dc, 0, 512), dc == 0, dc == 15,
                         waits=([(f"G{r}", wWg), ("V", LN2_V),
                                 ("A", ev_pair[r])] if dc == 0 else []),
                         inc="T" if dc == 15 else None)
            gwb_last_T[r] = prog.val("T")
            t_u = 0
            for dc in range(NDC):
                t_u = mm(ps[bU][:, 0:512], uwv[:, dc * 128:(dc + 1) * 128],
                         sl3(ln2_3, dc, 0, 512), dc == 0, dc == 15,
                         waits=[(f"U{r}", wWu)] if dc == 0 else [],
                         inc="T" if dc == 15 else None)
            uwb_last_T[r] = prog.val("T")
            sgb = V(o_sgb[r], 1024, BF16)
            a_si = ac("activation", sgb, ps[bG][:, 0:512], AF.Silu,
                      scale=sgv[:, ic:ic + 1], waits=[("T", t_g)], inc="A")
            ev_pair[r] = a_si
            ve("scalar_tensor_tensor", fuse_ic(ic), ps[bU][:, 0:512],
               suv[:, ic:ic + 1], sgb, ALU.mult, ALU.mult,
               waits=[("T", t_u), ("A", a_si)], inc="V")
        prog.emit("vector", lambda e: e.drain(), inc="V")
        FUSE_V = prog.val("V")
        FUSE_A = prog.val("A")
        if it == 0:
            stop_here("fuse", [
                ("fuse0", V(o_xf, 16384, BF16)), ("fuse1", V(o_q, 16384, BF16)),
                ("fuse2", V(o_k, 16384, BF16)), ("fuse3", V(o_v, 16384, BF16))])

        ev = 0
        dn_base_T = prog.val("T")
        wd_first = [[("V", PROJ_DONE_V)], [("V", PROJ_DONE_V)],
                    [("T", gwb_last_T[0])], [("T", gwb_last_T[1])],
                    [("T", uwb_last_T[0])], [("T", uwb_last_T[1])]]
        for pas in range(2):
            for ic in range(NIC):
                sl = (pas * NIC + ic) % 6
                wdv = V(wd_off[sl], 4096, BF16)
                if wd_first[sl] is not None:
                    w0d = wd_first[sl]
                    wd_first[sl] = None
                else:
                    w0d = [("T", wd_last_T[sl])]
                wWd = dma_w(wdv, wd8_in[ic * 128:(ic + 1) * 128, :],
                            waits=w0d, sem=WD_SEMS[sl])
                f3 = fuse_ic(ic).rearrange("p (a b) -> p a b", a=4)
                for tcl in range(2):
                    tc = 2 * pas + tcl
                    for oq in range(4):
                        bank = tcl * 4 + oq
                        w0 = []
                        if tcl == 0 and oq == 0:
                            w0 = [(WD_SEMS[sl], wWd)]
                        if ic == 0:
                            w0.append(("V", FUSE_V if pas == 0 else ev))
                            if pas == 0 and tcl == 0 and oq == 0:
                                w0.append(("A", FUSE_A))
                        mm(ps[bank][:, 0:512], sl3(f3, tc, 0, 128),
                           wdv[:, oq * 512:(oq + 1) * 512],
                           ic == 0, ic == NIC - 1, waits=w0,
                           inc=("T" if (tcl == 1 and oq == 3) else None))
                wd_last_T[sl] = prog.val("T")
            tw = prog.val("T")
            for tcl in range(2):
                tc = 2 * pas + tcl
                for oq in range(4):
                    bank = tcl * 4 + oq
                    col = tc * D + oq * 512
                    ev = ve("tensor_tensor", ffn_tm[:, col:col + 512],
                            ps[bank][:, 0:512],
                            sdb[:, oq * 512:(oq + 1) * 512], ALU.mult,
                            waits=[("T", tw)] if (tcl == 0 and oq == 0)
                            else [],
                            inc="V" if (tcl == 1 and oq == 3) else None)
            if pas == 1:
                ms["ps67_free_V"] = ev
            chain_tc = [2 * pas, 2 * pas + 1]
            a_f = ln_pair([(ffn_tm[:, t * D:(t + 1) * D], t, [("V", ev)])
                           for t in chain_tc])
            xvs = {}
            for i, t in enumerate(chain_tc):
                ff_s = ffn_tm[:, t * D:(t + 1) * D]
                ve("tensor_scalar", ff_s, ff_s, stat[:, 8 + t:9 + t],
                   stat[:, 12 + t:13 + t], ALU.mult, ALU.add,
                   waits=[("A", a_f[i])])
                ve("tensor_tensor", xs[t], xs[t], ff_s, ALU.add)
                v_add = prog.emit("vector", lambda e: e.drain(), inc="V")
                dma_st(out_hbm[r0 + t * 128:r0 + (t + 1) * 128, :], xs[t],
                       waits=[("V", v_add)])
                n_stores += 1
                if it + 1 < NTILE:
                    r0b = (it + 1) * PT
                    xvs[t] = dma_x(x2[:, t * D:(t + 1) * D],
                                   x_in[r0b + t * 128:r0b + (t + 1) * 128,
                                        :], sem="XI")
            if it + 1 < NTILE:
                a_l = ln_pair([(x2[:, t * D:(t + 1) * D], t,
                                [("XI", xvs[t])]) for t in chain_tc])
                for i, t in enumerate(chain_tc):
                    lt = V(o_g + t * 4096, 4096, BF16)
                    ln1_state[(it + 1, t)] = ve(
                        "tensor_scalar", lt, x2[:, t * D:(t + 1) * D],
                        stat[:, 8 + t:9 + t], stat[:, 12 + t:13 + t],
                        ALU.mult, ALU.add,
                        waits=[("A", a_l[i]), ("T", dn_base_T)], inc="V")
        DOWN_V = prog.val("V")
        if it == 0:
            stop_here("down", [("ffn", ffn_tm)])
        ms["ps03_free_V"] = ms["ps67_free_V"]
        ms["supd_free_V"] = DOWN_V
        ms["ps67_free2_V"] = DOWN_V
        ms["ffnA"] = prog.val("A")

    except _StopEmit:
        pass

    n_dbg = 0
    for nm, ap in dbg_specs:
        shp = list(ap.shape)
        flat = [shp[0], 1]
        for d in shp[1:]:
            flat[1] *= d
        o = nc.dram_tensor("dbg_" + nm, flat, F32,
                           kind="ExternalOutput").ap()
        wts = [("T", prog.val("T")), ("V", prog.val("V")),
               ("A", prog.val("A"))]

        def fdb(e, o=o, src=ap):
            return e.dma_start(out=o, in_=src)
        prog.emit("gpsimd", fdb, wts, "ST", 16)
        n_dbg += 1
    prog.emit("sync", lambda e: e.nop(),
              waits=[("ST", 16 * (n_stores + n_dbg))])

    blk = enter(nc.Block())
    prog.install(blk)
    for g in reversed(ctx):
        g.__exit__(None, None, None)
    prog.close()
    nc._prog_ops = prog.ops
    return nc


_NC_CACHE = None


def _get_nc():
    global _NC_CACHE
    if _NC_CACHE is None:
        _NC_CACHE = build()
    return _NC_CACHE


def _layout_weights(inp):
    c = np.ascontiguousarray
    dev = {}
    for nm, key in (("wq", "L0_attn_Wq"), ("wk", "L0_attn_Wk"),
                    ("wv", "L0_attn_Wv"), ("wg", "L0_attn_Wg"),
                    ("wo", "L0_attn_Wo")):
        W = np.asarray(inp[key])
        dev[nm] = c(W.T.reshape(NDC, 128, 2, 1024).transpose(2, 0, 1, 3))
    dev["w1"] = c(np.asarray(inp["L0_attn_maa_w1"])
                  .reshape(NDC, 128, 160).transpose(1, 0, 2)
                  .reshape(128, NDC * 160))
    dev["w2"] = c(np.asarray(inp["L0_attn_maa_w2"])
                  .transpose(1, 0, 2).reshape(32, 5 * D))
    dev["dw1"] = c(np.asarray(inp["L0_attn_decay_w1"])
                   .reshape(NDC, 128, 64).transpose(1, 0, 2)
                   .reshape(128, NDC * 64))
    dev["dw2"] = c(np.asarray(inp["L0_attn_decay_w2"]))
    maas = np.stack([np.asarray(inp[f"L0_attn_maa_{f}"]).reshape(D)
                     for f in "xrkvwg"])
    dev["maa6"] = c(maas.reshape(6, NDC, 128).transpose(2, 0, 1)
                    .reshape(128, 96))
    dev["td"] = c(np.asarray(inp["L0_attn_time_decay"]).reshape(NDC, 128).T)
    dev["bg"] = c(np.asarray(inp["L0_attn_bg"]).reshape(NDC, 128).T)
    dev["sg"] = c(np.asarray(inp["L0_ffn_s_gate"]).reshape(NIC, 128).T)
    dev["su"] = c(np.asarray(inp["L0_ffn_s_up"]).reshape(NIC, 128).T)
    dev["sd"] = c(np.asarray(inp["L0_ffn_s_down"]).reshape(1, D))
    for nm, key in (("wg8", "L0_ffn_w_gate"), ("wu8", "L0_ffn_w_up")):
        W = np.asarray(inp[key])
        dev[nm] = c(W.reshape(NDC, 128, NIC, 128).transpose(2, 1, 0, 3)
                    .reshape(NIC, 128, NDC * 128))
    dev["wd8"] = c(np.asarray(inp["L0_ffn_w_down"]))
    dev["idm"] = np.eye(128, dtype=np.float32)
    dev["maskm"] = np.triu(np.ones((128, 128), dtype=np.float32))
    return dev


def kernel(**inputs):
    nc = _get_nc()
    dev = _layout_weights(inputs)
    xall = np.asarray(inputs["L0_attn_hidden_states"])
    in_maps = []
    for b in range(8):
        m = dict(dev)
        m["x"] = np.ascontiguousarray(xall[b])
        in_maps.append(m)
    res = run_bass_kernel_spmd(nc, in_maps, core_ids=list(range(8)))
    return np.stack([res.results[b]["out"] for b in range(8)]).astype(np.float32)
